# version 05
# baseline (speedup 1.0000x reference)
import numpy as np
import re
import contextlib
import concourse.bass as bass
import concourse.mybir as mybir
from concourse.bass_utils import run_bass_kernel_spmd

F32 = mybir.dt.float32
BF16 = mybir.dt.bfloat16
AF = mybir.ActivationFunctionType
ALU = mybir.AluOpType
AX = mybir.AxisListType

T = 4096
D = 1024
DI = 2048
DS = 64


_OPND = re.compile(r"@([A-Za-z0-9_.\-]+?)(?:\+(\d+))?:\[((?:\[[^\]]*\](?:, )?)+)\]")
_PAIR = re.compile(r"\[(-?\d+), (-?\d+)\]")

NCNT = 6
CC = {"tensor": 0, "vector": 1, "scalar": 2, "gpsimd": 4}
QC = {"sync": 3, "gpsimd": 5}
ENG_NAMES = ("gpsimd", "sync", "tensor", "vector", "scalar")


def _parse_fps(concise, is_dma):
    k = concise.find(" in=[")
    outs_s = concise[: k if k >= 0 else len(concise)]
    ins_s = concise[k:] if k >= 0 else ""

    def parse(region):
        fps = []
        for m in _OPND.finditer(region):
            name = m.group(1)
            off = int(m.group(2) or 0)
            pairs = [(int(a), int(b)) for a, b in _PAIR.findall(m.group(3))]
            if is_dma or len(pairs) < 2 or pairs[0][0] <= 0:
                fps.append((name, None))
                continue
            pitch, nparts = pairs[0]
            plo = off // pitch
            flo = off % pitch
            ext = 1 + sum(abs(st) * (n - 1) for st, n in pairs[1:])
            if flo + ext > pitch:
                flo, fhi = 0, pitch
            else:
                fhi = flo + ext
            fps.append((name, (plo, plo + nparts, flo, fhi)))
        return fps

    return parse(outs_s), parse(ins_s)


def _ivover(a, b):
    if a is None or b is None:
        return True
    return a[0] < b[1] and b[0] < a[1] and a[2] < b[3] and b[2] < a[3]


def _compute_schedule(meta, caps):
    n = len(meta)
    cum = [0] * NCNT
    state = {e: [0] * NCNT for e in ENG_NAMES}
    last_vc_q = {3: [0] * NCNT, 5: [0] * NCNT}
    vc_lookup = [dict() for _ in range(NCNT)]
    acc = {}
    waits = [[] for _ in range(n)]
    MAXL = 24
    for i in range(n):
        eng, nd = meta[i]
        is_dma = nd > 0
        ctr = QC[eng] if is_dma else CC[eng]
        outs = []
        ins = []
        for cs in caps[i]:
            o, ii = _parse_fps(cs, is_dma)
            outs += o
            ins += ii
        need = [0] * NCNT
        for fp in ins:
            rec = acc.get(fp[0])
            if rec:
                for c in range(NCNT):
                    for iv, cv in rec[0][c]:
                        if cv > need[c] and _ivover(fp[1], iv):
                            need[c] = cv
        for fp in outs:
            rec = acc.get(fp[0])
            if rec:
                for c in range(NCNT):
                    for lst in (rec[0][c], rec[1][c]):
                        for iv, cv in lst:
                            if cv > need[c] and _ivover(fp[1], iv):
                                need[c] = cv
        st = state[eng]
        free = {0} if (eng == "tensor" and not is_dma) else set()
        for c in range(NCNT):
            if c in free:
                continue
            if need[c] > st[c]:
                waits[i].append((c, need[c]))
                vcw = vc_lookup[c][need[c]]
                st = [max(a, b) for a, b in zip(st, vcw)]
        cum_after = cum[ctr] + (16 * nd if is_dma else 1)
        cum[ctr] = cum_after
        vc_i = list(st)
        if is_dma:
            vc_i = [max(a, b) for a, b in zip(vc_i, last_vc_q[ctr])]
            vc_i[ctr] = cum_after
            last_vc_q[ctr] = vc_i
        else:
            vc_i[ctr] = cum_after
            if eng == "tensor":
                st = list(st)
                st[ctr] = cum_after
        state[eng] = st
        vc_lookup[ctr][cum_after] = vc_i
        for fps, which in ((outs, 0), (ins, 1)):
            for fp in fps:
                rec = acc.get(fp[0])
                if rec is None:
                    rec = ([[] for _ in range(NCNT)], [[] for _ in range(NCNT)])
                    acc[fp[0]] = rec
                lst = rec[which][ctr]
                lst.append((fp[1], cum_after))
                if len(lst) > MAXL:
                    mx = max(cv for _, cv in lst)
                    del lst[:]
                    lst.append((None, mx))
    return waits, cum


def build_program(nc, make_steps, record=False, plan=None):
    steps = []

    class P:
        def c(self, eng, fn):
            steps.append((eng, (lambda e, f=fn: [f(e)]), 0))

        def g(self, eng, fn):
            steps.append((eng, fn, 0))

        def d(self, eng, fn, n):
            steps.append((eng, fn, n))

    make_steps(P())

    if record:
        caps = [None] * len(steps)
        with nc.Block() as blk:
            def mk(name):
                def body(eng):
                    for idx, (e, fn, nd) in enumerate(steps):
                        if e != name:
                            continue
                        instrs = fn(eng)
                        if nd:
                            assert len(instrs) == nd, (name, len(instrs), nd)
                        caps[idx] = [i.concise() for i in instrs]
                return body

            blk.gpsimd(mk("gpsimd"))
            blk.sync(mk("sync"))
            blk.tensor(mk("tensor"))
            blk.vector(mk("vector"))
            blk.scalar(mk("scalar"))
        return [(e, nd) for e, _, nd in steps], caps

    if plan is not None:
        waits, totals = plan
        assert len(waits) == len(steps)
        with contextlib.ExitStack() as semstack:
            sems = [semstack.enter_context(nc.semaphore(f"dsem{c}")) for c in range(NCNT)]
            with nc.Block() as blk:
                def mk(name):
                    def body(eng):
                        for idx, (e, fn, nd) in enumerate(steps):
                            if e != name:
                                continue
                            for c, v in waits[idx]:
                                eng.wait_ge(sems[c], v)
                            instrs = fn(eng)
                            if nd:
                                assert len(instrs) == nd, (name, len(instrs), nd)
                                for ins in instrs:
                                    ins.then_inc(sems[QC[name]], 16)
                            else:
                                instrs[-1].then_inc(sems[CC[name]], 1)
                        if name == "scalar":
                            for c in range(NCNT):
                                if totals[c] > 0 and c != CC["scalar"]:
                                    eng.wait_ge(sems[c], totals[c])
                    return body

                blk.gpsimd(mk("gpsimd"))
                blk.sync(mk("sync"))
                blk.tensor(mk("tensor"))
                blk.vector(mk("vector"))
                blk.scalar(mk("scalar"))
        return nc

    totals = []
    t = 0
    for eng, fn, nd in steps:
        totals.append(t)
        t += 16 * nd if nd else 1
    final_total = t

    with nc.semaphore("gsem") as sem, nc.Block() as blk:
        def mk(name):
            def body(eng):
                for (e, fn, nd), tb in zip(steps, totals):
                    if e != name:
                        continue
                    if tb > 0:
                        eng.wait_ge(sem, tb)
                    instrs = fn(eng)
                    if nd:
                        assert len(instrs) == nd, (name, len(instrs), nd)
                        for ins in instrs:
                            ins.then_inc(sem, 16)
                    else:
                        instrs[-1].then_inc(sem, 1)
                if name == "scalar":
                    eng.wait_ge(sem, final_total)
            return body

        blk.gpsimd(mk("gpsimd"))
        blk.sync(mk("sync"))
        blk.tensor(mk("tensor"))
        blk.vector(mk("vector"))
        blk.scalar(mk("scalar"))
    return nc


class Ctx:
    def __init__(self, nc, stack):
        self.nc = nc
        sb = lambda n, s, d: stack.enter_context(nc.sbuf_tensor(n, s, d))
        ps = lambda n, s, d: stack.enter_context(nc.psum_tensor(n, s, d))
        self.P0 = ps("P0", [128, 1024], F32)
        self.P1 = ps("P1", [128, 1024], F32)
        self.P2 = ps("P2", [128, 1024], F32)
        self.P3 = ps("P3", [128, 1024], BF16)
        self.P4 = ps("P4", [128, 512], F32)
        self.ident = sb("ident", [128, 128], BF16)
        self.identf = sb("identf", [128, 128], F32)
        self.ones_c128_b = sb("ones_c128_b", [128, 1], BF16)
        self.ones_c64_f = sb("ones_c64_f", [64, 1], F32)
        self.ones_r128_f = sb("ones_r128_f", [1, 128], F32)
        self.tA = sb("tA", [128, 1024], F32)
        self.tB = sb("tB", [128, 1024], F32)
        self.tC = sb("tC", [128, 1024], F32)
        self.tD = sb("tD", [128, 1024], F32)
        self.col = sb("colscr", [128, 8], F32)
        self.kcol = sb("kcol", [128, 4], F32)

    def setup(self, p):
        p.g("gpsimd", lambda g: [
            g.iota(self.tA[:, 0:128], pattern=[[1, 128]], base=0, channel_multiplier=0,
                   allow_small_or_imprecise_dtypes=True),
            g.iota(self.tB[:, 0:128], pattern=[[0, 128]], base=0, channel_multiplier=1,
                   allow_small_or_imprecise_dtypes=True),
            g.memset(self.ones_c128_b[:], 1.0),
            g.memset(self.ones_c64_f[:], 1.0),
            g.memset(self.ones_r128_f[:], 1.0),
            g.memset(self.kcol[:, 0:1], 1.0),
            g.memset(self.kcol[:, 1:2], 1e-5),
            g.memset(self.kcol[:, 2:3], 1e-6),
            g.memset(self.kcol[:, 3:4], 1e-20),
        ])
        p.c("vector", lambda v: v.tensor_tensor(self.identf[:], self.tB[:, 0:128], self.tA[:, 0:128], ALU.is_equal))
        p.c("scalar", lambda s: s.copy(self.ident[:], self.identf[:]))

    def bcast_rows(self, p, dst, src_row, n, npart=128):
        for j0 in range(0, n, 512):
            w = min(512, n - j0)
            p.c("tensor", lambda t, j0=j0, w=w: t.matmul(
                self.P4[0:npart, 0:w], self.ones_r128_f[:, 0:npart],
                src_row[:, j0:j0 + w], start=True, stop=True))
            p.c("scalar", lambda s, j0=j0, w=w: s.copy(dst[:, j0:j0 + w], self.P4[0:npart, 0:w]))

    def transpose_tile(self, p, dst_view, src_bf, ncols):
        nb = ncols // 128
        def mms(t):
            return [t.transpose(self.P3[:, j * 128:(j + 1) * 128],
                                src_bf[:, j * 128:(j + 1) * 128], self.ident[:])
                    for j in range(nb)]
        p.g("tensor", mms)
        p3v = self.P3[:, 0:nb * 128].rearrange("p (n t) -> p n t", n=nb)
        p.c("scalar", lambda s: s.copy(dst_view, p3v))

    def softplus(self, p, out, x_in, bias_col, w):
        if bias_col is None:
            p.c("vector", lambda v: v.tensor_copy(self.tA[:, 0:w], x_in))
        else:
            p.c("vector", lambda v: v.tensor_scalar(self.tA[:, 0:w], x_in, bias_col, None, ALU.add))
        p.c("scalar", lambda s: s.activation(self.tB[:, 0:w], self.tA[:, 0:w], AF.Abs))
        p.c("scalar", lambda s: s.activation(self.tB[:, 0:w], self.tB[:, 0:w], AF.Exp, scale=-1.0))
        p.c("scalar", lambda s: s.activation(self.tB[:, 0:w], self.tB[:, 0:w], AF.Ln, bias=self.kcol[:, 0:1]))
        p.c("vector", lambda v: v.tensor_scalar_max(self.tA[:, 0:w], self.tA[:, 0:w], 0.0))
        p.c("vector", lambda v: v.tensor_add(out, self.tA[:, 0:w], self.tB[:, 0:w]))

    def layernorm_tile(self, p, out, x_f32, w, eps, g_tile=None, b_tile=None, rms=False):
        col = self.col
        mu = col[:, 0:1]; var = col[:, 1:2]; rstd = col[:, 2:3]
        if not rms:
            p.c("vector", lambda v: v.reduce_sum(mu, x_f32, axis=AX.X))
            p.c("scalar", lambda s: s.mul(mu, mu, 1.0 / w))
            p.c("vector", lambda v: v.tensor_scalar(self.tD[:, 0:w], x_f32, mu, None, ALU.subtract))
            src = self.tD[:, 0:w]
        else:
            src = x_f32
        p.c("scalar", lambda s: s.activation(self.tC[:, 0:w], src, AF.Square, accum_out=var))
        ecol = self.kcol[:, 1:2] if eps == 1e-5 else self.kcol[:, 2:3]
        p.c("scalar", lambda s: s.activation(rstd, var, AF.Ln, bias=ecol, scale=1.0 / w))
        p.c("scalar", lambda s: s.activation(rstd, rstd, AF.Exp, scale=-0.5))
        if g_tile is None:
            p.c("vector", lambda v: v.tensor_scalar(out, src, rstd, None, ALU.mult))
        elif b_tile is None:
            p.c("vector", lambda v: v.tensor_scalar(self.tC[:, 0:w], src, rstd, None, ALU.mult))
            p.c("vector", lambda v: v.tensor_mul(out, self.tC[:, 0:w], g_tile))
        else:
            p.c("vector", lambda v: v.tensor_scalar(self.tC[:, 0:w], src, rstd, None, ALU.mult))
            p.c("vector", lambda v: v.tensor_mul(self.tC[:, 0:w], self.tC[:, 0:w], g_tile))
            p.c("vector", lambda v: v.tensor_add(out, self.tC[:, 0:w], b_tile))


def make_l0_attn_steps(nc, cx, stack, p, x_d, y_d, W, prefix="", dbg=None):
    TH = 2048
    sb = lambda n, s, d: stack.enter_context(nc.sbuf_tensor(prefix + n, s, d))
    xnT = sb("xnT", [128, 8 * TH], BF16)
    xcT = sb("xcT", [128, 16 * TH], BF16)

    W_in_r = W["W_in"].rearrange("(n p) o -> p n o", p=128)
    W_dt_r = W["W_dt"].rearrange("(n p) o -> p n o", p=128)
    W_x_r = W["W_x"].rearrange("(n p) o -> p n o", p=128)


    convw = sb("convw", [128, 16, 4], F32)
    convb = sb("convb", [128, 16], F32)
    dcol = sb("dcol", [128, 16], F32)
    bdt_row = sb("bdt_row", [1, 512], BF16)
    ones_r128_b = sb("ones_r128_b", [1, 128], BF16)
    acol = sb("acol", [64, 1], F32)
    xpcarry = sb("xpcarry", [128, 16, 3], BF16)
    hcarry = sb("hcarry", [64, 1], F32)
    g_t = sb("g_t", [128, D], F32)
    b_t = sb("b_t", [128, D], F32)
    wxp = sb("wxp", [128, 16, 128], BF16)
    BstT = sb("BstT", [64, TH], F32)
    CstT = sb("CstT", [64, TH], F32)
    ysB = sb("ysB", [128, TH], BF16)

    p.d("gpsimd", lambda g: [
        g.dma_start(out=convw[:], in_=W["conv_w"].rearrange("(n p) k -> p n k", p=128),
                    allow_slow_non_contiguous=True),
        g.dma_start(out=convb[:], in_=W["conv_b"].rearrange("(n p) -> p n", p=128),
                    allow_slow_non_contiguous=True),
        g.dma_start(out=dcol[:], in_=W["D"].rearrange("(n p) -> p n", p=128),
                    allow_slow_non_contiguous=True),
        g.dma_start(out=acol[:], in_=W["A_log"].rearrange("(p o) -> p o", o=1),
                    allow_slow_non_contiguous=True),
        g.dma_start(out=wxp[:], in_=W_x_r),
    ], 5)
    p.c("scalar", lambda s: s.activation(acol[:], acol[:], AF.Exp))
    p.c("scalar", lambda s: s.mul(acol[:], acol[:], -1.0))
    p.c("gpsimd", lambda g: g.memset(ones_r128_b[:], 1.0))
    p.d("sync", lambda s: [
        s.dma_start(out=cx.tA[0:1, 0:D], in_=W["ln_g"].rearrange("(a b) -> a b", a=1)),
        s.dma_start(out=cx.tB[0:1, 0:D], in_=W["ln_b"].rearrange("(a b) -> a b", a=1))], 2)
    cx.bcast_rows(p, g_t, cx.tA[0:1, 0:D], D)
    cx.bcast_rows(p, b_t, cx.tB[0:1, 0:D], D)

    with contextlib.ExitStack() as ph:
        psb = lambda n, s, d: ph.enter_context(nc.sbuf_tensor(prefix + "ph" + n, s, d))
        xbf = psb("xbf", [128, D], BF16)
        pan = [psb("panelA", [128, 8, 128], BF16), psb("panelB", [128, 8, 128], BF16)]
        panf = [q[:, :, :].rearrange("p a b -> p (a b)") for q in pan]
        ca = psb("ca", [128, TH], F32)
        cb = psb("cb", [128, TH], F32)
        xpp = psb("xpp", [128, 3 + TH], BF16)
        wdtp = psb("wdtp", [128, 16, 512], BF16)
        dmacc = psb("dmacc", [128, 16], F32)
        spf = cx.tC
        drow = psb("drow", [1, TH], F32)
        sres = psb("sres", [128, 512], BF16)
        sres2 = psb("sres2", [128, 512], BF16)
        yT8 = wdtp
        AdT = ca[0:64, :]
        hT = cb[0:64, :]
        ysrow = drow

        for half in range(2):
            t_off = half * TH
            for tt in range(TH // 128):
                p.d("sync", lambda s, tt=tt, t_off=t_off: [s.dma_start(
                    out=cx.tA[:, 0:D], in_=x_d[t_off + tt * 128:t_off + (tt + 1) * 128, :])], 1)
                cx.layernorm_tile(p, xbf[:, 0:D], cx.tA[:, 0:D], D, 1e-5, g_tile=g_t[:, 0:D], b_tile=b_t[:, 0:D])
                dst = xnT[:, :].rearrange("p (n t) -> p n t", n=8)[:, :, tt * 128:(tt + 1) * 128]
                if dbg is not None and half == 0 and tt == 0:
                    p.d("sync", lambda s: [s.dma_start(out=dbg["xn"], in_=xbf[:, 0:D])], 1)
                cx.transpose_tile(p, dst, xbf[:, 0:D], D)

            for ic in range(16):
                h = ic % 2
                p.d("gpsimd", lambda s, ic=ic, h=h: [s.dma_start(
                    out=pan[h][:], in_=W_in_r[:, :, ic * 128:(ic + 1) * 128])], 1)
                for tch in range(TH // 512):
                    pm = cx.P0 if tch % 2 == 0 else cx.P1
                    def mms(t, ic=ic, tch=tch, h=h, pm=pm):
                        return [t.matmul(pm[:, 0:512], pan[h][:, dc],
                                         xnT[:, dc * TH + tch * 512: dc * TH + (tch + 1) * 512],
                                         start=(dc == 0), stop=(dc == 7)) for dc in range(8)]
                    p.g("tensor", mms)
                    p.c("scalar", lambda s, tch=tch, pm=pm: s.copy(
                        xpp[:, 3 + tch * 512:3 + (tch + 1) * 512], pm[:, 0:512]))
                if dbg is not None and half == 0 and ic == 0:
                    p.d("sync", lambda s: [s.dma_start(out=dbg["cw"], in_=convw[:, :, :].rearrange("p a b -> p (a b)")),
                                           s.dma_start(out=dbg["cb"], in_=convb[:]),
                                           s.dma_start(out=dbg["panel"], in_=panf[0]),
                                           s.dma_start(out=dbg["xp"], in_=xpp[:, 3:3 + TH]),
                                           s.dma_start(out=dbg["wxp"], in_=wxp[:, :, :].rearrange("p a b -> p (a b)"))], 5)
                if half == 0:
                    p.c("gpsimd", lambda g: g.memset(xpp[:, 0:3], 0.0))
                else:
                    p.c("vector", lambda v, ic=ic: v.tensor_copy(xpp[:, 0:3], xpcarry[:, ic]))
                p.c("vector", lambda v, ic=ic: v.tensor_copy(xpcarry[:, ic], xpp[:, TH:TH + 3]))
                w0, w1, w2, w3 = [convw[:, ic, k:k + 1] for k in range(4)]
                cbv = convb[:, ic:ic + 1]
                p.c("vector", lambda v, w3=w3, cbv=cbv: v.tensor_scalar(
                    ca[:, 0:TH], xpp[:, 3:3 + TH], w3, cbv, ALU.mult, ALU.add))
                p.c("vector", lambda v, w2=w2: v.scalar_tensor_tensor(
                    cb[:, 0:TH], xpp[:, 2:2 + TH], w2, ca[:, 0:TH], ALU.mult, ALU.add))
                p.c("vector", lambda v, w1=w1: v.scalar_tensor_tensor(
                    ca[:, 0:TH], xpp[:, 1:1 + TH], w1, cb[:, 0:TH], ALU.mult, ALU.add))
                p.c("vector", lambda v, w0=w0: v.scalar_tensor_tensor(
                    cb[:, 0:TH], xpp[:, 0:TH], w0, ca[:, 0:TH], ALU.mult, ALU.add))
                p.c("scalar", lambda s, ic=ic: s.activation(
                    xcT[:, ic * TH:(ic + 1) * TH], cb[:, 0:TH], AF.Silu))
                if dbg is not None and half == 0 and ic == 0:
                    p.d("sync", lambda s: [s.dma_start(out=dbg["xc"], in_=xcT[:, 0:TH]),
                                           s.dma_start(out=dbg["xnT"], in_=xnT[:, 0:TH])], 2)

            for tch in range(TH // 512):
                pm = cx.P0 if tch % 2 == 0 else cx.P1
                def mms(t, tch=tch, pm=pm):
                    return [t.matmul(pm[:, 0:512], wxp[:, icc],
                                     xcT[:, icc * TH + tch * 512:icc * TH + (tch + 1) * 512],
                                     start=(icc == 0), stop=(icc == 15)) for icc in range(16)]
                p.g("tensor", mms)
                p.c("vector", lambda v, tch=tch, pm=pm: v.tensor_copy(
                    BstT[:, tch * 512:(tch + 1) * 512], pm[0:64, 0:512]))
                p.c("vector", lambda v, tch=tch, pm=pm: v.tensor_copy(
                    CstT[:, tch * 512:(tch + 1) * 512], pm[64:128, 0:512]))

            for ocg in range(4):
                p.d("gpsimd", lambda s, ocg=ocg: [s.dma_start(
                    out=wdtp[:], in_=W_dt_r[:, :, ocg * 512:(ocg + 1) * 512]),
                    s.dma_start(out=bdt_row[:], in_=W["b_dt"].rearrange(
                        "(a b) -> a b", a=1)[:, ocg * 512:(ocg + 1) * 512])], 2)
                for tt in range(TH // 128):
                    pm = cx.P0 if tt % 2 == 0 else cx.P1
                    def mms(t, ocg=ocg, tt=tt, pm=pm):
                        out = [t.matmul(pm[:, 0:512],
                                        xcT[:, icc * TH + tt * 128:icc * TH + (tt + 1) * 128],
                                        wdtp[:, icc], start=(icc == 0), stop=False)
                               for icc in range(16)]
                        out.append(t.matmul(pm[:, 0:512], ones_r128_b[:],
                                            bdt_row[:, 0:512],
                                            start=False, stop=True))
                        return out
                    p.g("tensor", mms)
                    cx.softplus(p, spf[:, 0:512], pm[:, 0:512], None, 512)
                    if ocg == 0:
                        p.c("vector", lambda v, tt=tt: v.reduce_sum(
                            dmacc[:, tt:tt + 1], spf[:, 0:512], axis=AX.X))
                    else:
                        p.c("vector", lambda v: v.reduce_sum(cx.col[:, 3:4], spf[:, 0:512], axis=AX.X))
                        p.c("vector", lambda v, tt=tt: v.tensor_add(
                            dmacc[:, tt:tt + 1], dmacc[:, tt:tt + 1], cx.col[:, 3:4]))
            if dbg is not None and half == 0:
                p.d("sync", lambda s: [s.dma_start(out=dbg["B"], in_=BstT[:]),
                                       s.dma_start(out=dbg["C"], in_=CstT[:])], 2)
            p.c("scalar", lambda s: s.mul(dmacc[:], dmacc[:], 1.0 / DI))
            if dbg is not None and half == 0:
                p.d("sync", lambda s: [s.dma_start(out=dbg["dm"], in_=dmacc[:])], 1)
            for tt in range(TH // 128):
                p.c("tensor", lambda t, tt=tt: t.matmul(
                    cx.P4[0:1, 0:128], dmacc[:, tt:tt + 1], cx.identf[:], start=True, stop=True))
                p.c("scalar", lambda s, tt=tt: s.copy(drow[0:1, tt * 128:(tt + 1) * 128], cx.P4[0:1, 0:128]))
            for j0 in range(0, TH, 512):
                p.c("tensor", lambda t, j0=j0: t.matmul(
                    cx.P4[0:64, 0:512], cx.ones_r128_f[:, 0:64],
                    drow[:, j0:j0 + 512], start=True, stop=True))
                p.c("vector", lambda v, j0=j0: v.tensor_scalar(
                    AdT[:, j0:j0 + 512], cx.P4[0:64, 0:512], acol[:], None, ALU.mult))
            p.c("scalar", lambda s: s.activation(AdT[:], AdT[:], AF.Exp))
            if half == 0:
                p.c("vector", lambda v: v.tensor_tensor_scan(
                    hT[:], AdT[:], BstT[:, :], 0.0, ALU.mult, ALU.add))
            else:
                p.c("vector", lambda v: v.tensor_tensor_scan(
                    hT[:], AdT[:], BstT[:, :], hcarry[:], ALU.mult, ALU.add))
            p.c("vector", lambda v: v.tensor_copy(hcarry[:], hT[:, TH - 1:TH]))
            p.c("vector", lambda v: v.tensor_mul(hT[:], hT[:], CstT[:, :]))
            for j0 in range(0, TH, 512):
                p.c("tensor", lambda t, j0=j0: t.matmul(
                    cx.P4[0:1, 0:512], cx.ones_c64_f[:], hT[:, j0:j0 + 512], start=True, stop=True))
                p.c("scalar", lambda s, j0=j0: s.copy(ysrow[0:1, j0:j0 + 512], cx.P4[0:1, 0:512]))
            cx.bcast_rows(p, ysB, ysrow, TH)
            if dbg is not None and half == 0:
                p.d("sync", lambda s: [s.dma_start(out=dbg["ys"], in_=ysB[:])], 1)

            for t8 in range(TH // 512):
                t0 = t8 * 512
                for ic in range(16):
                    h = ic % 2
                    pm2 = cx.P2[:, 0:512] if h == 0 else cx.P2[:, 512:1024]
                    sr = sres if h == 0 else sres2
                    sc5 = cx.tA if h == 0 else cx.tB
                    p.d("gpsimd", lambda s, ic=ic, h=h: [s.dma_start(
                        out=pan[h][:],
                        in_=W_in_r[:, :, 2048 + ic * 128:2048 + (ic + 1) * 128])], 1)
                    def mms(t, ic=ic, h=h, t0=t0, pm2=pm2):
                        return [t.matmul(pm2, pan[h][:, dc],
                                         xnT[:, dc * TH + t0: dc * TH + t0 + 512],
                                         start=(dc == 0), stop=(dc == 7)) for dc in range(8)]
                    p.g("tensor", mms)
                    p.c("scalar", lambda s, sr=sr, pm2=pm2: s.activation(sr[:], pm2, AF.Silu))
                    p.c("vector", lambda v, ic=ic, t0=t0, sc5=sc5: v.scalar_tensor_tensor(
                        sc5[:, 0:512], xcT[:, ic * TH + t0:ic * TH + t0 + 512],
                        dcol[:, ic:ic + 1], ysB[:, t0:t0 + 512], ALU.mult, ALU.add))
                    p.c("vector", lambda v, ic=ic, sr=sr, sc5=sc5: v.tensor_mul(
                        yT8[:, ic], sc5[:, 0:512], sr[:]))
                for rnd in range(2):
                    for ic in range(16):
                        h = ic % 2
                        p.d("gpsimd", lambda s, ic=ic, h=h: [s.dma_start(
                            out=panf[h][:],
                            in_=W["W_out"][ic * 128:(ic + 1) * 128, :])], 1)
                        def mms(t, ic=ic, h=h, rnd=rnd):
                            out = []
                            for tc in range(2):
                                pm = cx.P0 if tc == 0 else cx.P1
                                for ccn in range(2):
                                    out.append(t.matmul(
                                        pm[:, ccn * 512:(ccn + 1) * 512],
                                        yT8[:, ic, (rnd * 2 + tc) * 128:(rnd * 2 + tc + 1) * 128],
                                        panf[h][:, ccn * 512:(ccn + 1) * 512],
                                        start=(ic == 0), stop=(ic == 15)))
                            return out
                        p.g("tensor", mms)
                    for tc in range(2):
                        pm = cx.P0 if tc == 0 else cx.P1
                        trow = t_off + t0 + (rnd * 2 + tc) * 128
                        p.d("sync", lambda s, trow=trow: [s.dma_start(
                            out=cx.tB[:, 0:D], in_=x_d[trow:trow + 128, :])], 1)
                        p.c("vector", lambda v, pm=pm: v.scalar_tensor_tensor(
                            cx.tA[:, 0:D], cx.tB[:, 0:D], 2.0, pm[:, 0:D], ALU.mult, ALU.add))
                        cx.layernorm_tile(p, cx.tB[:, 0:D], cx.tA[:, 0:D], D, 1e-5)
                        p.d("sync", lambda s, trow=trow: [s.dma_start(
                            out=y_d[trow:trow + 128, :], in_=cx.tB[:, 0:D])], 1)


def build_l0_attn(debug=False):
    nc = bass.Bass("TRN2", target_bir_lowering=False, debug=False)
    x_d = nc.dram_tensor("x", [T, D], F32, kind="ExternalInput").ap()
    names = [("ln_g", [D]), ("ln_b", [D]), ("W_in", [D, 2 * DI]), ("conv_w", [DI, 4]),
             ("conv_b", [DI]), ("W_x", [DI, 2 * DS]), ("W_dt", [DI, DI]), ("b_dt", [DI]),
             ("A_log", [DS]), ("D", [DI]), ("W_out", [DI, D])]
    W = {n: nc.dram_tensor(n, s, F32, kind="ExternalInput").ap() for n, s in names}
    y_d = nc.dram_tensor("y", [T, D], F32, kind="ExternalOutput").ap()
    dbg = None
    if debug:
        dbg = {"xn": nc.dram_tensor("dbg_xn", [128, D], BF16, kind="ExternalOutput").ap(),
               "xc": nc.dram_tensor("dbg_xc", [128, 2048], BF16, kind="ExternalOutput").ap(),
               "xnT": nc.dram_tensor("dbg_xnT", [128, 2048], BF16, kind="ExternalOutput").ap(),
               "B": nc.dram_tensor("dbg_B", [64, 2048], F32, kind="ExternalOutput").ap(),
               "C": nc.dram_tensor("dbg_C", [64, 2048], F32, kind="ExternalOutput").ap(),
               "dm": nc.dram_tensor("dbg_dm", [128, 16], F32, kind="ExternalOutput").ap(),
               "ys": nc.dram_tensor("dbg_ys", [128, 2048], BF16, kind="ExternalOutput").ap(),
               "panel": nc.dram_tensor("dbg_panel", [128, 1024], BF16, kind="ExternalOutput").ap(),
               "xp": nc.dram_tensor("dbg_xp", [128, 2048], BF16, kind="ExternalOutput").ap(),
               "wxp": nc.dram_tensor("dbg_wxp", [128, 2048], BF16, kind="ExternalOutput").ap(),
               "cw": nc.dram_tensor("dbg_cw", [128, 64], F32, kind="ExternalOutput").ap(),
               "cb": nc.dram_tensor("dbg_cb", [128, 16], F32, kind="ExternalOutput").ap()}
    with contextlib.ExitStack() as stack:
        cx = Ctx(nc, stack)
        build_program(nc, lambda p: (cx.setup(p), make_l0_attn_steps(nc, cx, stack, p, x_d, y_d, W, dbg=dbg)))
    return nc




def add_ctx_attn(nc, cx, stack, p, pfx=""):
    sb = lambda n, s, d: stack.enter_context(nc.sbuf_tensor(pfx + n, s, d))
    cx.ptA = sb("ptA", [128, 512], BF16)
    cx.ptB = sb("ptB", [128, 512], BF16)
    cx.ptA2 = sb("ptA2", [128, 512], BF16)
    cx.ptB2 = sb("ptB2", [128, 512], BF16)
    cx.dnrow = sb("dnrow", [1, 512], F32)
    cx.dnb = sb("dnb", [64, 512], F32)
    cx.stgA = sb("stgA", [64, 512], BF16)
    cx.stgB = sb("stgB", [64, 512], BF16)


def load_transposed(nc, cx, p, dst, w_dram, out_dim, in_dim, stages):
    nic = in_dim // 128
    for oc in range(out_dim // 128):
        stage = stages[oc % 2]
        p.d("gpsimd", lambda g, oc=oc, stage=stage: [g.dma_start(
            out=stage[:, 0:in_dim], in_=w_dram[oc * 128:(oc + 1) * 128, :])], 1)
        for icg in range(0, nic, 8):
            nb = min(8, nic - icg)
            def mms(t, icg=icg, nb=nb, stage=stage):
                return [t.transpose(cx.P3[:, j * 128:(j + 1) * 128],
                                    stage[:, (icg + j) * 128:(icg + j + 1) * 128], cx.ident[:])
                        for j in range(nb)]
            p.g("tensor", mms)
            dv = dst[:, icg:icg + nb, oc * 128:(oc + 1) * 128]
            p3v = cx.P3[:, 0:nb * 128].rearrange("p (n t) -> p n t", n=nb)
            p.c("scalar", lambda s, dv=dv, p3v=p3v: s.copy(dv, p3v))


def attn_one_hp(nc, cx, p, hp, qT_hp, kT_hp, vaug, attn_dram, T_, causal, masks):
    dnrow, dnb = cx.dnrow, cx.dnb
    stgA, stgB = cx.stgA, cx.stgB
    NQ = T_ // 512
    KCALL = T_ // 128
    for qb in range(NQ):
        kmax = min(KCALL, 4 * (qb + 1)) if causal else KCALL
        for kc in range(kmax):
            pm = cx.P0 if kc % 2 == 0 else cx.P1
            pa = cx.ptA if kc % 2 == 0 else cx.ptA2
            pb = cx.ptB if kc % 2 == 0 else cx.ptB2
            def smm(t, qb=qb, kc=kc, pm=pm):
                return [
                    t.matmul(pm[:, 0:512], kT_hp[0:64, kc * 128:(kc + 1) * 128],
                             qT_hp[0:64, qb * 512:(qb + 1) * 512], start=True, stop=True),
                    t.matmul(pm[:, 512:1024], kT_hp[64:128, kc * 128:(kc + 1) * 128],
                             qT_hp[64:128, qb * 512:(qb + 1) * 512], start=True, stop=True)]
            p.g("tensor", smm)
            p.c("scalar", lambda s, pa=pa, pm=pm: s.activation(pa[:], pm[:, 0:512], AF.Exp, scale=0.125))
            p.c("scalar", lambda s, pb=pb, pm=pm: s.activation(pb[:], pm[:, 512:1024], AF.Exp, scale=0.125))
            if causal and kc >= 4 * qb:
                o = (kc - 4 * qb)
                p.c("vector", lambda v, o=o, pa=pa: v.tensor_mul(pa[:], pa[:], masks[o][:]))
                p.c("vector", lambda v, o=o, pb=pb: v.tensor_mul(pb[:], pb[:], masks[o][:]))
            def pvmm(t, qb=qb, kc=kc, kmax=kmax, pa=pa, pb=pb):
                return [
                    t.matmul(cx.P2[0:65, 0:512], vaug[:, 0, kc, :], pa[:],
                             start=(kc == 0), stop=(kc == kmax - 1)),
                    t.matmul(cx.P2[0:65, 512:1024], vaug[:, 1, kc, :], pb[:],
                             start=(kc == 0), stop=(kc == kmax - 1))]
            p.g("tensor", pvmm)
        for h, off in ((0, 0), (1, 512)):
            stg = stgA if h == 0 else stgB
            p.c("vector", lambda v, off=off: v.reciprocal(dnrow[:], cx.P2[64:65, off:off + 512]))
            p.c("tensor", lambda t: t.matmul(cx.P4[0:64, 0:512], cx.ones_r128_f[:, 0:64],
                                             dnrow[:], start=True, stop=True))
            p.c("scalar", lambda s: s.copy(dnb[:], cx.P4[0:64, 0:512]))
            p.c("vector", lambda v, off=off, stg=stg: v.tensor_mul(
                stg[:], cx.P2[0:64, off:off + 512], dnb[:]))
            p.d("sync", lambda s, h=h, stg=stg, qb=qb, hp=hp: [s.dma_start(
                out=attn_dram[hp * 128 + h * 64:hp * 128 + h * 64 + 64,
                              qb * 512:(qb + 1) * 512], in_=stg[:])], 1)


def build_causal_masks(nc, cx, stack, p):
    sb = lambda n, s, d: stack.enter_context(nc.sbuf_tensor(n, s, d))
    masks = [sb(f"mask{o}", [128, 512], BF16) for o in range(4)]
    cj = sb("mcj", [128, 512], F32)
    rp = sb("mrp", [128, 512], F32)
    p.g("gpsimd", lambda g: [
        g.iota(cj[:], pattern=[[1, 512]], base=0, channel_multiplier=0,
               allow_small_or_imprecise_dtypes=True),
        g.iota(rp[:], pattern=[[0, 512]], base=0, channel_multiplier=1,
               allow_small_or_imprecise_dtypes=True)])
    for o in range(4):
        p.c("vector", lambda v, o=o: v.tensor_single_scalar(cx.tA[:, 0:512], rp[:], float(o * 128), ALU.add))
        p.c("vector", lambda v, o=o: v.tensor_tensor(masks[o][:], cx.tA[:, 0:512], cj[:], ALU.is_le))
    return masks


def make_l1_attn_steps(nc, cx, stack, p, x_d, y_d, W):
    sb = lambda n, s, d: stack.enter_context(nc.sbuf_tensor(n, s, d))
    add_ctx_attn(nc, cx, stack, p)
    masks = build_causal_masks(nc, cx, stack, p)
    attn_dram = nc.dram_tensor("attn_scr", [1024, T], BF16, kind="Internal").ap()

    Wq_r = W["Wq"].rearrange("(n p) h o -> p n (h o)", p=128)
    Wkvd_r = W["Wkv_down"].rearrange("(n p) o -> p n o", p=128)
    Wku_r = W["Wk_up"].rearrange("(n p) h o -> p n (h o)", p=128)
    Wvu_r = W["Wv_up"].rearrange("(n p) h o -> p n (h o)", p=128)
    Wout_r = W["Wout"].rearrange("(n h2) o d -> (h2 o) n d", h2=2)

    wq = sb("wq", [128, 8, 1024], BF16)
    wkvd = sb("wkvd", [128, 8, 256], BF16)
    wku = sb("wku", [128, 2, 1024], BF16)
    wvu = sb("wvu", [128, 2, 1024], BF16)
    wout = sb("wout", [128, 8, 1024], BF16)
    latT = sb("latT", [128, 2 * T], BF16)
    qT = sb("qTl1", [128, T], BF16)
    kT = sb("kTl1", [128, T], BF16)
    vaug = sb("vaugl1", [128, 2, T // 128, 65], BF16)
    xbf = sb("xbfl1", [128, D], BF16)

    p.d("gpsimd", lambda g: [
        g.dma_start(out=wq[:], in_=Wq_r),
        g.dma_start(out=wkvd[:], in_=Wkvd_r),
        g.dma_start(out=wku[:], in_=Wku_r),
        g.dma_start(out=wvu[:], in_=Wvu_r),
        g.dma_start(out=wout[:], in_=Wout_r),
    ], 5)

    with contextlib.ExitStack() as ph:
        psb = lambda n, s, d: ph.enter_context(nc.sbuf_tensor("l1a" + n, s, d))
        xT = psb("xT", [128, 8 * T], BF16)
        for tt in range(T // 128):
            p.d("sync", lambda s, tt=tt: [s.dma_start(out=cx.tA[:, 0:D],
                                                      in_=x_d[tt * 128:(tt + 1) * 128, :])], 1)
            p.c("scalar", lambda s: s.copy(xbf[:, 0:D], cx.tA[:, 0:D]))
            dst = xT[:, :].rearrange("p (n t) -> p n t", n=8)[:, :, tt * 128:(tt + 1) * 128]
            cx.transpose_tile(p, dst, xbf[:, 0:D], D)
        for lc in range(2):
            for tch in range(T // 512):
                pm = cx.P0 if tch % 2 == 0 else cx.P1
                def mms(t, lc=lc, tch=tch, pm=pm):
                    return [t.matmul(pm[:, 0:512], wkvd[:, dc, lc * 128:(lc + 1) * 128],
                                     xT[:, dc * T + tch * 512:dc * T + (tch + 1) * 512],
                                     start=(dc == 0), stop=(dc == 7)) for dc in range(8)]
                p.g("tensor", mms)
                p.c("scalar", lambda s, lc=lc, tch=tch, pm=pm: s.copy(
                    latT[:, lc * T + tch * 512:lc * T + (tch + 1) * 512], pm[:, 0:512]))
        for hp in range(8):
            for tch in range(T // 512):
                pm = cx.P0 if tch % 2 == 0 else cx.P1
                def mms(t, hp=hp, tch=tch, pm=pm):
                    return [t.matmul(pm[:, 0:512], wq[:, dc, hp * 128:(hp + 1) * 128],
                                     xT[:, dc * T + tch * 512:dc * T + (tch + 1) * 512],
                                     start=(dc == 0), stop=(dc == 7)) for dc in range(8)]
                p.g("tensor", mms)
                p.c("scalar", lambda s, tch=tch, pm=pm: s.copy(
                    qT[:, tch * 512:(tch + 1) * 512], pm[:, 0:512]))
                def mmsk(t, hp=hp, tch=tch, pm=pm):
                    return [t.matmul(pm[:, 0:512], wku[:, lc, hp * 128:(hp + 1) * 128],
                                     latT[:, lc * T + tch * 512:lc * T + (tch + 1) * 512],
                                     start=(lc == 0), stop=(lc == 1)) for lc in range(2)]
                p.g("tensor", mmsk)
                p.c("scalar", lambda s, tch=tch, pm=pm: s.copy(
                    kT[:, tch * 512:(tch + 1) * 512], pm[:, 0:512]))
            p.c("gpsimd", lambda g: g.memset(vaug[:, :, :, 64:65], 1.0))
            for tc in range(T // 128):
                def mmsv(t, hp=hp, tc=tc):
                    return [t.matmul(cx.P2[:, 0:128], latT[:, lc * T + tc * 128:lc * T + (tc + 1) * 128],
                                     wvu[:, lc, hp * 128:(hp + 1) * 128],
                                     start=(lc == 0), stop=(lc == 1)) for lc in range(2)]
                p.g("tensor", mmsv)
                p.c("scalar", lambda s, tc=tc: s.copy(vaug[:, 0, tc, 0:64], cx.P2[:, 0:64]))
                p.c("scalar", lambda s, tc=tc: s.copy(vaug[:, 1, tc, 0:64], cx.P2[:, 64:128]))
            attn_one_hp(nc, cx, p, hp, qT, kT, vaug, attn_dram, T, True, masks)

    with contextlib.ExitStack() as ph:
        psb = lambda n, s, d: ph.enter_context(nc.sbuf_tensor("l1o" + n, s, d))
        atile = psb("atile", [128, 8, 128], BF16)
        for tc in range(T // 128):
            p.d("sync", lambda s, tc=tc: [s.dma_start(
                out=atile[:],
                in_=attn_dram.rearrange("(n p) t -> p n t", p=128)[:, :, tc * 128:(tc + 1) * 128])], 1)
            def mms(t, tc=tc):
                out = []
                for hpc in range(8):
                    for cc in range(2):
                        out.append(t.matmul(cx.P1[:, cc * 512:(cc + 1) * 512],
                                            atile[:, hpc, :],
                                            wout[:, hpc, cc * 512:(cc + 1) * 512],
                                            start=(hpc == 0), stop=(hpc == 7)))
                return out
            p.g("tensor", mms)
            p.d("sync", lambda s, tc=tc: [s.dma_start(
                out=cx.tB[:, 0:D], in_=x_d[tc * 128:(tc + 1) * 128, :])], 1)
            p.c("vector", lambda v: v.tensor_add(cx.tA[:, 0:D], cx.tB[:, 0:D], cx.P1[:, 0:D]))
            cx.layernorm_tile(p, cx.tB[:, 0:D], cx.tA[:, 0:D], D, 1e-5)
            p.d("sync", lambda s, tc=tc: [s.dma_start(
                out=y_d[tc * 128:(tc + 1) * 128, :], in_=cx.tB[:, 0:D])], 1)


def build_l1_attn():
    nc = bass.Bass("TRN2", target_bir_lowering=False, debug=False)
    x_d = nc.dram_tensor("x", [T, D], F32, kind="ExternalInput").ap()
    W = {"Wq": nc.dram_tensor("Wq", [D, 16, 64], F32, kind="ExternalInput").ap(),
         "Wkv_down": nc.dram_tensor("Wkv_down", [D, 256], F32, kind="ExternalInput").ap(),
         "Wk_up": nc.dram_tensor("Wk_up", [256, 16, 64], F32, kind="ExternalInput").ap(),
         "Wv_up": nc.dram_tensor("Wv_up", [256, 16, 64], F32, kind="ExternalInput").ap(),
         "Wout": nc.dram_tensor("Wout", [16, 64, D], F32, kind="ExternalInput").ap()}
    y_d = nc.dram_tensor("y", [T, D], F32, kind="ExternalOutput").ap()
    with contextlib.ExitStack() as stack:
        cx = Ctx(nc, stack)
        build_program(nc, lambda p: (cx.setup(p), make_l1_attn_steps(nc, cx, stack, p, x_d, y_d, W)))
    return nc


def make_l1_ffn_steps(nc, cx, stack, p, x_d, y_d, W):
    sb = lambda n, s, d: stack.enter_context(nc.sbuf_tensor(n, s, d))
    E = 8; LAT = 512; INTER = 512; SH = 4096
    routed_dram = nc.dram_tensor("routed_scr", [T, D], F32, kind="Internal").ap()
    upT_dram = nc.dram_tensor("upT_scr", [D, SH], BF16, kind="Internal").ap()

    xfT = sb("xfT", [128, 8 * T], BF16)
    wtab = sb("wtab", [128, T // 128, 8], F32)
    comb = sb("comb", [128, T // 128, 512], BF16)
    xbf = sb("xbff", [128, D], BF16)
    stage = sb("stagew", [128, 4096], BF16)
    stageB = sb("stagewB", [128, 1024], BF16)
    stg2 = [stage, stageB]
    cbB = sb("cbB", [128, 8], F32)
    ip8 = sb("ip8", [128, 8], F32)
    ones8 = sb("ones8", [128, 8], F32)
    gateT = sb("gateT", [128, 8, 8], BF16)
    sc8 = sb("sc8", [128, 8], F32)
    sf8 = sb("sf8", [128, 8], F32)
    e8 = sb("e8", [128, 8], F32)

    for tt in range(T // 128):
        p.d("sync", lambda s, tt=tt: [s.dma_start(out=cx.tA[:, 0:D],
                                                  in_=x_d[tt * 128:(tt + 1) * 128, :])], 1)
        p.c("scalar", lambda s: s.copy(xbf[:, 0:D], cx.tA[:, 0:D]))
        dst = xfT[:, :].rearrange("p (n t) -> p n t", n=8)[:, :, tt * 128:(tt + 1) * 128]
        cx.transpose_tile(p, dst, xbf[:, 0:D], D)

    p.d("gpsimd", lambda g: [g.dma_start(out=stage[0:8, 0:1024], in_=W["gate"][:, :]),
                             g.dma_start(out=cx.tD[0:1, 0:8], in_=W["corr_bias"].rearrange("(a b) -> a b", a=1))], 2)
    for dc in range(8):
        p.c("tensor", lambda t, dc=dc: t.transpose(cx.P3[:, 0:8], stage[0:8, dc * 128:(dc + 1) * 128], cx.ident[0:8, 0:8]))
        p.c("scalar", lambda s, dc=dc: s.copy(gateT[:, dc, :], cx.P3[:, 0:8]))
    cx.bcast_rows(p, cbB, cx.tD[0:1, 0:8], 8)
    p.g("gpsimd", lambda g: [g.iota(ip8[:], pattern=[[1, 8]], base=0, channel_multiplier=0,
                                    allow_small_or_imprecise_dtypes=True),
                             g.memset(ones8[:], 1.0)])
    p.c("scalar", lambda s: s.mul(ip8[:], ip8[:], 1e-18))
    t8a = cx.col[:, 0:1]; t8b = cx.col[:, 1:2]; t8c = cx.col[:, 2:3]; t8d = cx.col[:, 3:4]
    for tt in range(T // 128):
        def mms(t, tt=tt):
            return [t.matmul(cx.P4[:, 0:8], xfT[:, dc * T + tt * 128:dc * T + (tt + 1) * 128],
                             gateT[:, dc, :], start=(dc == 0), stop=(dc == 7)) for dc in range(8)]
        p.g("tensor", mms)
        p.c("scalar", lambda s: s.activation(sc8[:], cx.P4[:, 0:8], AF.Sigmoid))
        p.c("vector", lambda v: v.tensor_add(sf8[:], sc8[:], cbB[:]))
        for gi, sl in ((0, slice(0, 4)), (1, slice(4, 8))):
            m1 = cx.col[:, 4 + gi:5 + gi]
            p.c("vector", lambda v, sl=sl, m1=m1: v.reduce_max(m1, sf8[:, sl], axis=AX.X))
            p.c("vector", lambda v, sl=sl, m1=m1: v.tensor_scalar(e8[:, sl], sf8[:, sl], m1, None, ALU.is_ge))
            p.c("vector", lambda v, sl=sl: v.scalar_tensor_tensor(e8[:, sl], e8[:, sl], -1e30, sf8[:, sl], ALU.mult, ALU.add))
            m2 = t8a if gi == 0 else t8b
            p.c("vector", lambda v, sl=sl, m2=m2: v.reduce_max(m2, e8[:, sl], axis=AX.X))
        gs0 = cx.col[:, 6:7]; gs1 = cx.col[:, 7:8]
        p.c("vector", lambda v: v.tensor_add(gs0, cx.col[:, 4:5], t8a))
        p.c("vector", lambda v: v.tensor_add(gs1, cx.col[:, 5:6], t8b))
        p.c("vector", lambda v: v.tensor_tensor(t8c, gs0, gs1, ALU.is_ge))
        p.c("vector", lambda v: v.tensor_scalar(t8d, t8c, -1.0, 1.0, ALU.mult, ALU.add))
        p.c("vector", lambda v: v.tensor_scalar(e8[:, 0:4], ones8[:, 0:4], t8c, None, ALU.mult))
        p.c("vector", lambda v: v.tensor_scalar(e8[:, 4:8], ones8[:, 4:8], t8d, None, ALU.mult))
        p.c("vector", lambda v: v.tensor_mul(sf8[:], sf8[:], e8[:]))
        p.c("vector", lambda v: v.tensor_sub(sf8[:], sf8[:], ip8[:]))
        p.c("vector", lambda v: v.reduce_max(t8a, sf8[:], axis=AX.X))
        p.c("vector", lambda v: v.tensor_scalar(e8[:], sf8[:], t8a, None, ALU.is_ge))
        p.c("vector", lambda v: v.scalar_tensor_tensor(e8[:], e8[:], -1e30, sf8[:], ALU.mult, ALU.add))
        p.c("vector", lambda v: v.reduce_max(t8b, e8[:], axis=AX.X))
        p.c("vector", lambda v: v.tensor_scalar(e8[:], sf8[:], t8b, None, ALU.is_ge))
        p.c("vector", lambda v: v.tensor_mul(e8[:], e8[:], sc8[:]))
        p.c("vector", lambda v: v.reduce_sum(t8c, e8[:], axis=AX.X))
        p.c("vector", lambda v: v.tensor_scalar(t8c, t8c, cx.kcol[:, 3:4], None, ALU.add))
        p.c("vector", lambda v: v.reciprocal(t8d, t8c))
        p.c("vector", lambda v, tt=tt: v.tensor_scalar(wtab[:, tt, :], e8[:], t8d, None, ALU.mult))

    with contextlib.ExitStack() as ph:
        psb = lambda n, s, d: ph.enter_context(nc.sbuf_tensor("l1f" + n, s, d))
        fc1T = psb("fc1T", [128, 8, 512], BF16)
        latT = psb("latT", [128, 4 * T], BF16)
        h1T = psb("h1T", [128, 4 * T], BF16)
        w1eT = psb("w1eT", [128, 4, 512], BF16)
        w2eT = psb("w2eT", [128, 4, 512], BF16)
        load_transposed(nc, cx, p, fc1T, W["fc1_w"], 512, 1024, stg2)
        for Lc in range(4):
            for tch in range(T // 512):
                pm = cx.P0 if tch % 2 == 0 else cx.P1
                def mms(t, Lc=Lc, tch=tch, pm=pm):
                    return [t.matmul(pm[:, 0:512], fc1T[:, dc, Lc * 128:(Lc + 1) * 128],
                                     xfT[:, dc * T + tch * 512:dc * T + (tch + 1) * 512],
                                     start=(dc == 0), stop=(dc == 7)) for dc in range(8)]
                p.g("tensor", mms)
                p.c("scalar", lambda s, Lc=Lc, tch=tch, pm=pm: s.copy(
                    latT[:, Lc * T + tch * 512:Lc * T + (tch + 1) * 512], pm[:, 0:512]))
        p.c("vector", lambda v: v.memset(comb[:], 0.0))
        for e in range(E):
            load_transposed(nc, cx, p, w1eT, W["w1"][e], INTER, LAT, stg2)
            load_transposed(nc, cx, p, w2eT, W["w2"][e], LAT, INTER, stg2)
            for icc in range(4):
                for tch in range(T // 512):
                    pm = cx.P0 if tch % 2 == 0 else cx.P1
                    def mms(t, icc=icc, tch=tch, pm=pm):
                        return [t.matmul(pm[:, 0:512], w1eT[:, lc, icc * 128:(icc + 1) * 128],
                                         latT[:, lc * T + tch * 512:lc * T + (tch + 1) * 512],
                                         start=(lc == 0), stop=(lc == 3)) for lc in range(4)]
                    p.g("tensor", mms)
                    p.c("vector", lambda v, pm=pm: v.tensor_scalar_max(cx.tA[:, 0:512], pm[:, 0:512], 0.0))
                    p.c("scalar", lambda s, icc=icc, tch=tch: s.activation(
                        h1T[:, icc * T + tch * 512:icc * T + (tch + 1) * 512],
                        cx.tA[:, 0:512], AF.Square))
            for tt in range(T // 128):
                def mms(t, tt=tt):
                    return [t.matmul(cx.P2[:, 0:512], h1T[:, icc * T + tt * 128:icc * T + (tt + 1) * 128],
                                     w2eT[:, icc, :], start=(icc == 0), stop=(icc == 3))
                            for icc in range(4)]
                p.g("tensor", mms)
                p.c("vector", lambda v, tt=tt, e=e: v.scalar_tensor_tensor(
                    comb[:, tt, :], cx.P2[:, 0:512], wtab[:, tt, e:e + 1],
                    comb[:, tt, :], ALU.mult, ALU.add))

    with contextlib.ExitStack() as ph:
        psb = lambda n, s, d: ph.enter_context(nc.sbuf_tensor("l1g" + n, s, d))
        fc2T = psb("fc2T", [128, 4, 1024], BF16)
        ctile = psb("ctile", [128, 4, 128], BF16)
        load_transposed(nc, cx, p, fc2T, W["fc2_w"], 1024, 512, stg2)
        for tt in range(T // 128):
            def tmms(t, tt=tt):
                return [t.transpose(cx.P3[:, j * 128:(j + 1) * 128],
                                    comb[:, tt, j * 128:(j + 1) * 128], cx.ident[:]) for j in range(4)]
            p.g("tensor", tmms)
            p.c("scalar", lambda s: s.copy(ctile[:], cx.P3[:, 0:512].rearrange("p (n t) -> p n t", n=4)))
            def mms(t, tt=tt):
                out = []
                for lc in range(4):
                    for cc in range(2):
                        out.append(t.matmul(cx.P1[:, cc * 512:(cc + 1) * 512], ctile[:, lc, :],
                                            fc2T[:, lc, cc * 512:(cc + 1) * 512],
                                            start=(lc == 0), stop=(lc == 3)))
                return out
            p.g("tensor", mms)
            p.c("vector", lambda v: v.tensor_copy(cx.tC[:, 0:D], cx.P1[:, 0:D]))
            p.d("sync", lambda s, tt=tt: [s.dma_start(out=routed_dram[tt * 128:(tt + 1) * 128, :],
                                                      in_=cx.tC[:, 0:D])], 1)

    with contextlib.ExitStack() as ph:
        psb = lambda n, s, d: ph.enter_context(nc.sbuf_tensor("l1h" + n, s, d))
        dwT = psb("dwT", [128, 32, 1024], BF16)
        upc = psb("upc", [128, 8, 128], BF16)
        stc = psb("stc", [128, 2, 256], BF16)
        upT_r = upT_dram.rearrange("(n p) s -> p n s", p=128)
        for sc in range(32):
            stg = stg2[sc % 2]
            p.d("gpsimd", lambda g, sc=sc, stg=stg: [g.dma_start(
                out=stg[:, 0:1024], in_=W["up_w"][sc * 128:(sc + 1) * 128, :])], 1)
            def tmms(t, stg=stg):
                return [t.transpose(cx.P3[:, j * 128:(j + 1) * 128],
                                    stg[:, j * 128:(j + 1) * 128], cx.ident[:]) for j in range(8)]
            p.g("tensor", tmms)
            p.c("scalar", lambda s: s.copy(upc[:], cx.P3[:, 0:1024].rearrange("p (n t) -> p n t", n=8)))
            p.d("sync", lambda s, sc=sc: [s.dma_start(out=upT_r[:, :, sc * 128:(sc + 1) * 128],
                                                      in_=upc[:])], 1)
        load_transposed(nc, cx, p, dwT, W["down_w"], 1024, 4096, [stage, stage])
        for tg in range(T // 256):
            for sc in range(32):
                p.d("gpsimd", lambda g, sc=sc: [g.dma_start(
                    out=upc[:], in_=upT_r[:, :, sc * 128:(sc + 1) * 128])], 1)
                def mms(t, tg=tg, sc=sc):
                    return [t.matmul(cx.P2[:, 0:256], upc[:, dc, :],
                                     xfT[:, dc * T + tg * 256:dc * T + (tg + 1) * 256],
                                     start=(dc == 0), stop=(dc == 7)) for dc in range(8)]
                p.g("tensor", mms)
                p.c("vector", lambda v: v.tensor_scalar_max(cx.tA[:, 0:256], cx.P2[:, 0:256], 0.0))
                p.c("scalar", lambda s: s.activation(stc[:, 0, :], cx.tA[:, 0:256], AF.Square))
                def dmms(t, sc=sc):
                    out = []
                    for tc in range(2):
                        pm = cx.P0 if tc == 0 else cx.P1
                        for cc in range(2):
                            out.append(t.matmul(pm[:, cc * 512:(cc + 1) * 512],
                                                stc[:, 0, tc * 128:(tc + 1) * 128],
                                                dwT[:, sc, cc * 512:(cc + 1) * 512],
                                                start=(sc == 0), stop=(sc == 31)))
                    return out
                p.g("tensor", dmms)
            for tc in range(2):
                pm = cx.P0 if tc == 0 else cx.P1
                trow = tg * 256 + tc * 128
                p.d("sync", lambda s, trow=trow: [
                    s.dma_start(out=cx.tB[:, 0:D], in_=x_d[trow:trow + 128, :]),
                    s.dma_start(out=cx.tD[:, 0:D], in_=routed_dram[trow:trow + 128, :])], 2)
                p.c("vector", lambda v, pm=pm: v.tensor_add(cx.tA[:, 0:D], cx.tB[:, 0:D], pm[:, 0:D]))
                p.c("vector", lambda v: v.tensor_add(cx.tA[:, 0:D], cx.tA[:, 0:D], cx.tD[:, 0:D]))
                cx.layernorm_tile(p, cx.tB[:, 0:D], cx.tA[:, 0:D], D, 1e-5)
                p.d("sync", lambda s, trow=trow: [s.dma_start(
                    out=y_d[trow:trow + 128, :], in_=cx.tB[:, 0:D])], 1)


def build_l1_ffn():
    nc = bass.Bass("TRN2", target_bir_lowering=False, debug=False)
    x_d = nc.dram_tensor("x", [T, D], F32, kind="ExternalInput").ap()
    W = {"gate": nc.dram_tensor("gate", [8, D], F32, kind="ExternalInput").ap(),
         "corr_bias": nc.dram_tensor("corr_bias", [8], F32, kind="ExternalInput").ap(),
         "w1": nc.dram_tensor("w1", [8, 512, 512], F32, kind="ExternalInput").ap(),
         "w2": nc.dram_tensor("w2", [8, 512, 512], F32, kind="ExternalInput").ap(),
         "fc1_w": nc.dram_tensor("fc1_w", [512, D], F32, kind="ExternalInput").ap(),
         "fc2_w": nc.dram_tensor("fc2_w", [D, 512], F32, kind="ExternalInput").ap(),
         "up_w": nc.dram_tensor("up_w", [4096, D], F32, kind="ExternalInput").ap(),
         "down_w": nc.dram_tensor("down_w", [D, 4096], F32, kind="ExternalInput").ap()}
    y_d = nc.dram_tensor("y", [T, D], F32, kind="ExternalOutput").ap()
    with contextlib.ExitStack() as stack:
        cx = Ctx(nc, stack)
        build_program(nc, lambda p: (cx.setup(p), make_l1_ffn_steps(nc, cx, stack, p, x_d, y_d, W)))
    return nc


def make_l0_ffn_steps(nc, cx, stack, p, x_d, y_d, W):
    sb = lambda n, s, d: stack.enter_context(nc.sbuf_tensor(n, s, d))
    attn_dram = nc.dram_tensor("attn_scr0", [1024, T], BF16, kind="Internal").ap()
    x1_dram = nc.dram_tensor("x1_scr", [T, D], F32, kind="Internal").ap()
    acc_dram = nc.dram_tensor("acc_scr", [T, D], F32, kind="Internal").ap()

    wtab = sb("wtab0", [128, T // 128, 8], F32)
    xbf = sb("xbf0", [128, D], BF16)
    stage = sb("stagew0", [128, 2048], BF16)
    stageB = sb("stagew0B", [128, 1024], BF16)
    stg2 = [stage, stageB]
    PI = 3.14159265358979
    with contextlib.ExitStack() as ph:
        psb = lambda n, s, d: ph.enter_context(nc.sbuf_tensor("l0f" + n, s, d))
        add_ctx_attn(nc, cx, ph, p, pfx="f0")
        nw1 = psb("nw1", [128, D], F32)
        cosT2 = psb("cosT2", [128, T], BF16)
        sinsT2 = psb("sinsT2", [128, T], BF16)
        perm = psb("perm", [128, 128], BF16)
        rcol = psb("rcol", [128, 1], F32)
        invf = psb("invf", [128, 1], F32)
        sgn = psb("sgn", [128, 1], F32)
        p.g("gpsimd", lambda g: [
            g.iota(rcol[:], pattern=[[0, 1]], base=0, channel_multiplier=1,
                   allow_small_or_imprecise_dtypes=True)])
        ge64 = cx.col[:, 4:5]; r64 = cx.col[:, 5:6]; tmpc = cx.col[:, 6:7]
        p.c("vector", lambda v: v.tensor_single_scalar(ge64, rcol[:], 64.0, ALU.is_ge))
        p.c("vector", lambda v: v.scalar_tensor_tensor(r64, ge64, -64.0, rcol[:], ALU.mult, ALU.add))
        p.c("vector", lambda v: v.tensor_single_scalar(tmpc, r64, 32.0, ALU.is_ge))
        p.c("vector", lambda v: v.scalar_tensor_tensor(invf[:], tmpc, -32.0, r64, ALU.mult, ALU.add))
        p.c("scalar", lambda s: s.activation(invf[:], invf[:], AF.Exp, scale=-float(np.log(10000.0) / 32.0)))
        p.c("vector", lambda v: v.tensor_single_scalar(sgn[:], r64, 32.0, ALU.is_ge))
        p.c("vector", lambda v: v.tensor_scalar(sgn[:], sgn[:], 2.0, -1.0, ALU.mult, ALU.add))
        p.c("vector", lambda v: v.tensor_single_scalar(cx.tA[:, 0:1], r64, 32.0, ALU.add))
        p.c("vector", lambda v: v.tensor_single_scalar(cx.tB[:, 0:1], cx.tA[:, 0:1], 64.0, ALU.is_ge))
        p.c("vector", lambda v: v.scalar_tensor_tensor(cx.tA[:, 0:1], cx.tB[:, 0:1], -64.0, cx.tA[:, 0:1], ALU.mult, ALU.add))
        p.c("vector", lambda v: v.scalar_tensor_tensor(cx.tA[:, 0:1], ge64, 64.0, cx.tA[:, 0:1], ALU.mult, ALU.add))
        p.c("gpsimd", lambda g: g.iota(cx.tB[:, 0:128], pattern=[[1, 128]], base=0, channel_multiplier=0,
                                       allow_small_or_imprecise_dtypes=True))
        p.c("vector", lambda v: v.tensor_scalar(cx.tC[:, 0:128], cx.tB[:, 0:128], cx.tA[:, 0:1], None, ALU.is_equal))
        p.c("scalar", lambda s: s.copy(perm[:], cx.tC[:, 0:128]))
        for j0 in range(0, T, 1024):
            p.c("gpsimd", lambda g, j0=j0: g.iota(
                cx.tD[:, 0:1024], pattern=[[1, 1024]], base=j0, channel_multiplier=0,
                allow_small_or_imprecise_dtypes=True))
            for kind in range(2):
                ph0 = PI / 2 if kind == 0 else 0.0
                p.c("vector", lambda v, j0=j0: v.tensor_scalar(cx.tA[:, 0:1024], cx.tD[:, 0:1024],
                                                               invf[:], None, ALU.mult))
                if ph0:
                    p.c("vector", lambda v: v.tensor_single_scalar(cx.tA[:, 0:1024], cx.tA[:, 0:1024], PI / 2, ALU.add))
                p.c("vector", lambda v: v.tensor_scalar(cx.tB[:, 0:1024], cx.tA[:, 0:1024],
                                                        float(1.0 / (2 * PI)), 8388608.0 - 0.5, ALU.mult, ALU.add))
                p.c("vector", lambda v: v.tensor_single_scalar(cx.tB[:, 0:1024], cx.tB[:, 0:1024], -8388608.0, ALU.add))
                p.c("vector", lambda v: v.scalar_tensor_tensor(cx.tA[:, 0:1024], cx.tB[:, 0:1024], -2 * PI,
                                                               cx.tA[:, 0:1024], ALU.mult, ALU.add))
                p.c("vector", lambda v: v.tensor_single_scalar(cx.tB[:, 0:1024], cx.tA[:, 0:1024], PI, ALU.is_ge))
                p.c("vector", lambda v: v.scalar_tensor_tensor(cx.tA[:, 0:1024], cx.tB[:, 0:1024], -2 * PI,
                                                               cx.tA[:, 0:1024], ALU.mult, ALU.add))
                if kind == 0:
                    p.c("scalar", lambda s, j0=j0: s.activation(cosT2[:, j0:j0 + 1024], cx.tA[:, 0:1024], AF.Sin))
                else:
                    p.c("scalar", lambda s, j0=j0: s.activation(cx.tC[:, 0:1024], cx.tA[:, 0:1024], AF.Sin))
                    p.c("vector", lambda v, j0=j0: v.tensor_scalar(sinsT2[:, j0:j0 + 1024], cx.tC[:, 0:1024],
                                                                   sgn[:], None, ALU.mult))
        p.d("sync", lambda s: [s.dma_start(out=cx.tA[0:1, 0:D], in_=W["norm1_w"].rearrange("(a b) -> a b", a=1))], 1)
        cx.bcast_rows(p, nw1, cx.tA[0:1, 0:D], D)
        xn1T = psb("xn1T", [128, 8 * T], BF16)
        qkvT = psb("qkvT", [128, 8, 3072], BF16)
        qb_all = psb("qb_all", [128, 24], F32)
        qT = psb("qT0", [128, T], BF16)
        kT = psb("kT0", [128, T], BF16)
        vaug = psb("vaug0", [128, 2, T // 128, 65], BF16)
        qraw = psb("qraw", [128, 512], BF16)
        vbt = psb("vbt", [128, 128], F32)
        for tt in range(T // 128):
            p.d("sync", lambda s, tt=tt: [s.dma_start(out=cx.tA[:, 0:D],
                                                      in_=x_d[tt * 128:(tt + 1) * 128, :])], 1)
            cx.layernorm_tile(p, xbf[:, 0:D], cx.tA[:, 0:D], D, 1e-6, g_tile=nw1[:, 0:D], rms=True)
            dst = xn1T[:, :].rearrange("p (n t) -> p n t", n=8)[:, :, tt * 128:(tt + 1) * 128]
            cx.transpose_tile(p, dst, xbf[:, 0:D], D)
        load_transposed(nc, cx, p, qkvT, W["qkv_w"], 3072, 1024, stg2)
        p.d("gpsimd", lambda g: [g.dma_start(out=qb_all[:], in_=W["qkv_b"].rearrange("(n p) -> p n", p=128),
                                             allow_slow_non_contiguous=True)], 1)
        for hp in range(8):
            for which, dstT, col0 in ((0, qT, hp), (1, kT, 8 + hp)):
                for tch in range(T // 512):
                    def mms(t, col0=col0, tch=tch):
                        return [t.matmul(cx.P0[:, 0:512], qkvT[:, dc, col0 * 128:(col0 + 1) * 128],
                                         xn1T[:, dc * T + tch * 512:dc * T + (tch + 1) * 512],
                                         start=(dc == 0), stop=(dc == 7)) for dc in range(8)]
                    p.g("tensor", mms)
                    p.c("scalar", lambda s, col0=col0: s.add(qraw[:], cx.P0[:, 0:512],
                                                             qb_all[:, col0:col0 + 1]))
                    p.c("tensor", lambda t: t.matmul(cx.P1[0:128, 0:512], perm[:], qraw[:],
                                                     start=True, stop=True))
                    p.c("vector", lambda v, tch=tch: v.tensor_mul(cx.tA[:, 0:512], qraw[:],
                                                                  cosT2[:, tch * 512:(tch + 1) * 512]))
                    p.c("vector", lambda v, tch=tch: v.tensor_mul(cx.tB[:, 0:512], cx.P1[:, 0:512],
                                                                  sinsT2[:, tch * 512:(tch + 1) * 512]))
                    p.c("vector", lambda v, dstT=dstT, tch=tch: v.tensor_add(
                        dstT[:, tch * 512:(tch + 1) * 512], cx.tA[:, 0:512], cx.tB[:, 0:512]))
            p.c("gpsimd", lambda g: g.memset(vaug[:, :, :, 64:65], 1.0))
            p.d("sync", lambda s, hp=hp: [s.dma_start(
                out=cx.tD[0:1, 0:128],
                in_=W["qkv_b"].rearrange("(a b) -> a b", a=1)[:, 2048 + hp * 128:2048 + (hp + 1) * 128])], 1)
            cx.bcast_rows(p, vbt, cx.tD[0:1, 0:128], 128)
            for tc in range(T // 128):
                def mmsv(t, hp=hp, tc=tc):
                    return [t.matmul(cx.P2[:, 0:128], xn1T[:, dc * T + tc * 128:dc * T + (tc + 1) * 128],
                                     qkvT[:, dc, 2048 + hp * 128:2048 + (hp + 1) * 128],
                                     start=(dc == 0), stop=(dc == 7)) for dc in range(8)]
                p.g("tensor", mmsv)
                p.c("vector", lambda v: v.tensor_add(cx.tA[:, 0:128], cx.P2[:, 0:128], vbt[:]))
                p.c("scalar", lambda s, tc=tc: s.copy(vaug[:, 0, tc, 0:64], cx.tA[:, 0:64]))
                p.c("scalar", lambda s, tc=tc: s.copy(vaug[:, 1, tc, 0:64], cx.tA[:, 64:128]))
            attn_one_hp(nc, cx, p, hp, qT, kT, vaug, attn_dram, T, False, None)

    with contextlib.ExitStack() as ph:
        psb = lambda n, s, d: ph.enter_context(nc.sbuf_tensor("l0o" + n, s, d))
        owT = psb("owT", [128, 8, 1024], BF16)
        obt = psb("obt", [128, D], F32)
        atile = psb("atile0", [128, 8, 128], BF16)
        load_transposed(nc, cx, p, owT, W["out_w"], 1024, 1024, stg2)
        p.d("sync", lambda s: [s.dma_start(out=cx.tA[0:1, 0:D], in_=W["out_b"].rearrange("(a b) -> a b", a=1))], 1)
        cx.bcast_rows(p, obt, cx.tA[0:1, 0:D], D)
        for tc in range(T // 128):
            p.d("sync", lambda s, tc=tc: [s.dma_start(
                out=atile[:],
                in_=attn_dram.rearrange("(n p) t -> p n t", p=128)[:, :, tc * 128:(tc + 1) * 128])], 1)
            def mms(t, tc=tc):
                out = []
                for hpc in range(8):
                    for cc in range(2):
                        out.append(t.matmul(cx.P1[:, cc * 512:(cc + 1) * 512],
                                            atile[:, hpc, :],
                                            owT[:, hpc, cc * 512:(cc + 1) * 512],
                                            start=(hpc == 0), stop=(hpc == 7)))
                return out
            p.g("tensor", mms)
            p.d("sync", lambda s, tc=tc: [s.dma_start(
                out=cx.tB[:, 0:D], in_=x_d[tc * 128:(tc + 1) * 128, :])], 1)
            p.c("vector", lambda v: v.tensor_add(cx.tA[:, 0:D], cx.tB[:, 0:D], cx.P1[:, 0:D]))
            p.c("vector", lambda v: v.tensor_add(cx.tA[:, 0:D], cx.tA[:, 0:D], obt[:, 0:D]))
            p.d("sync", lambda s, tc=tc: [s.dma_start(
                out=x1_dram[tc * 128:(tc + 1) * 128, :], in_=cx.tA[:, 0:D])], 1)

    with contextlib.ExitStack() as ph:
        psb = lambda n, s, d: ph.enter_context(nc.sbuf_tensor("l0m" + n, s, d))
        xn2T = psb("xn2T", [128, 8 * T], BF16)
        nw2 = psb("nw2", [128, D], F32)
        gateT = psb("gateT0", [128, 8, 8], BF16)
        l8 = psb("l8", [128, 8], F32)
        e80 = psb("e80", [128, 8], F32)
        p.d("sync", lambda s: [s.dma_start(out=cx.tA[0:1, 0:D], in_=W["norm2_w"].rearrange("(a b) -> a b", a=1))], 1)
        cx.bcast_rows(p, nw2, cx.tA[0:1, 0:D], D)
        for tt in range(T // 128):
            p.d("sync", lambda s, tt=tt: [s.dma_start(out=cx.tA[:, 0:D],
                                                      in_=x1_dram[tt * 128:(tt + 1) * 128, :])], 1)
            cx.layernorm_tile(p, xbf[:, 0:D], cx.tA[:, 0:D], D, 1e-6, g_tile=nw2[:, 0:D], rms=True)
            dst = xn2T[:, :].rearrange("p (n t) -> p n t", n=8)[:, :, tt * 128:(tt + 1) * 128]
            cx.transpose_tile(p, dst, xbf[:, 0:D], D)
        p.d("gpsimd", lambda g: [g.dma_start(out=stage[0:8, 0:1024], in_=W["gate_w"][:, :])], 1)
        for dc in range(8):
            p.c("tensor", lambda t, dc=dc: t.transpose(cx.P3[:, 0:8], stage[0:8, dc * 128:(dc + 1) * 128], cx.ident[0:8, 0:8]))
            p.c("scalar", lambda s, dc=dc: s.copy(gateT[:, dc, :], cx.P3[:, 0:8]))
        t8a = cx.col[:, 0:1]; t8b = cx.col[:, 1:2]; t8c = cx.col[:, 2:3]; t8d = cx.col[:, 3:4]
        for tt in range(T // 128):
            def mms(t, tt=tt):
                return [t.matmul(cx.P4[:, 0:8], xn2T[:, dc * T + tt * 128:dc * T + (tt + 1) * 128],
                                 gateT[:, dc, :], start=(dc == 0), stop=(dc == 7)) for dc in range(8)]
            p.g("tensor", mms)
            p.c("vector", lambda v: v.tensor_copy(l8[:], cx.P4[:, 0:8]))
            p.c("vector", lambda v: v.reduce_max(t8a, l8[:], axis=AX.X, negate=True))
            p.c("vector", lambda v: v.tensor_scalar(e80[:], l8[:], t8a, None, ALU.add))
            p.c("vector", lambda v: v.tensor_single_scalar(cx.tA[:, 0:8], e80[:], 0.0, ALU.is_ge))
            p.c("vector", lambda v: v.scalar_tensor_tensor(cx.tA[:, 0:8], cx.tA[:, 0:8], -1e30, l8[:], ALU.mult, ALU.add))
            p.c("vector", lambda v: v.reduce_max(t8b, cx.tA[:, 0:8], axis=AX.X))
            p.c("vector", lambda v: v.tensor_scalar(cx.tB[:, 0:8], l8[:], t8b, None, ALU.is_ge))
            p.c("scalar", lambda s: s.activation(e80[:], e80[:], AF.Exp))
            p.c("vector", lambda v: v.tensor_add(t8c, t8b, t8a))
            p.c("scalar", lambda s: s.activation(t8c, t8c, AF.Exp))
            p.c("vector", lambda v: v.tensor_scalar(t8c, t8c, cx.kcol[:, 0:1], None, ALU.add))
            p.c("vector", lambda v: v.reciprocal(t8d, t8c))
            p.c("vector", lambda v: v.tensor_mul(e80[:], e80[:], cx.tB[:, 0:8]))
            p.c("vector", lambda v, tt=tt: v.tensor_scalar(wtab[:, tt, :], e80[:], t8d, None, ALU.mult))
        for tt in range(T // 128):
            p.d("sync", lambda s, tt=tt: [
                s.dma_start(out=cx.tA[:, 0:D], in_=x_d[tt * 128:(tt + 1) * 128, :]),
                s.dma_start(out=cx.tB[:, 0:D], in_=x1_dram[tt * 128:(tt + 1) * 128, :])], 2)
            p.c("vector", lambda v: v.tensor_add(cx.tC[:, 0:D], cx.tA[:, 0:D], cx.tB[:, 0:D]))
            p.d("sync", lambda s, tt=tt: [s.dma_start(out=acc_dram[tt * 128:(tt + 1) * 128, :],
                                                      in_=cx.tC[:, 0:D])], 1)
        with contextlib.ExitStack() as ph2:
            psb2 = lambda n, s, d: ph2.enter_context(nc.sbuf_tensor("l0e" + n, s, d))
            w1eT = psb2("w1eT0", [128, 8, 2048], BF16)
            w2eT = psb2("w2eT0", [128, 8, 2048], BF16)
            w3eT = psb2("w3eT0", [128, 16, 1024], BF16)
            b1c = psb2("b1c", [128, 16], F32)
            b2c = psb2("b2c", [128, 16], F32)
            b3t = nw2
            hTg = psb2("hTg", [128, 16, 512], BF16)
            for e in range(8):
                load_transposed(nc, cx, p, w1eT, W["w1"][e], 2048, 1024, stg2)
                load_transposed(nc, cx, p, w2eT, W["w2"][e], 2048, 1024, stg2)
                load_transposed(nc, cx, p, w3eT, W["w3"][e], 1024, 2048, [stage, stage])
                p.d("gpsimd", lambda g, e=e: [
                    g.dma_start(out=b1c[:], in_=W["b1"][e].rearrange("(n p) -> p n", p=128),
                                allow_slow_non_contiguous=True),
                    g.dma_start(out=b2c[:], in_=W["b2"][e].rearrange("(n p) -> p n", p=128),
                                allow_slow_non_contiguous=True),
                    g.dma_start(out=cx.tD[0:1, 0:D], in_=W["b3"][e].rearrange("(a b) -> a b", a=1))], 3)
                cx.bcast_rows(p, b3t, cx.tD[0:1, 0:D], D)
                for g8 in range(T // 512):
                    g0 = g8 * 512
                    for hc in range(16):
                        def mms1(t, hc=hc, g0=g0):
                            return [t.matmul(cx.P0[:, 0:512], w1eT[:, dc, hc * 128:(hc + 1) * 128],
                                             xn2T[:, dc * T + g0:dc * T + g0 + 512],
                                             start=(dc == 0), stop=(dc == 7)) for dc in range(8)]
                        p.g("tensor", mms1)
                        def mms2(t, hc=hc, g0=g0):
                            return [t.matmul(cx.P1[:, 0:512], w2eT[:, dc, hc * 128:(hc + 1) * 128],
                                             xn2T[:, dc * T + g0:dc * T + g0 + 512],
                                             start=(dc == 0), stop=(dc == 7)) for dc in range(8)]
                        p.g("tensor", mms2)
                        p.c("scalar", lambda s, hc=hc: s.activation(cx.tA[:, 0:512], cx.P0[:, 0:512],
                                                                    AF.Silu, bias=b1c[:, hc:hc + 1]))
                        p.c("scalar", lambda s, hc=hc: s.add(cx.tB[:, 0:512], cx.P1[:, 0:512],
                                                             b2c[:, hc:hc + 1]))
                        p.c("vector", lambda v, hc=hc: v.tensor_mul(hTg[:, hc, :], cx.tA[:, 0:512], cx.tB[:, 0:512]))
                    for ttl in range(4):
                        tglob = g8 * 4 + ttl
                        def mms3(t, ttl=ttl):
                            out = []
                            for hc in range(16):
                                for cc in range(2):
                                    out.append(t.matmul(cx.P2[:, cc * 512:(cc + 1) * 512],
                                                        hTg[:, hc, ttl * 128:(ttl + 1) * 128],
                                                        w3eT[:, hc, cc * 512:(cc + 1) * 512],
                                                        start=(hc == 0), stop=(hc == 15)))
                            return out
                        p.g("tensor", mms3)
                        p.c("vector", lambda v: v.tensor_add(cx.tA[:, 0:D], cx.P2[:, 0:D], b3t[:, 0:D]))
                        p.c("vector", lambda v, tglob=tglob, e=e: v.tensor_scalar(
                            cx.tB[:, 0:D], cx.tA[:, 0:D], wtab[:, tglob, e:e + 1], None, ALU.mult))
                        p.d("gpsimd", lambda g, tglob=tglob: [g.dma_start(
                            out=acc_dram[tglob * 128:(tglob + 1) * 128, :], in_=cx.tB[:, 0:D],
                            accum_op=ALU.add)], 1)
        for tt in range(T // 128):
            p.d("sync", lambda s, tt=tt: [s.dma_start(out=cx.tA[:, 0:D],
                                                      in_=acc_dram[tt * 128:(tt + 1) * 128, :])], 1)
            cx.layernorm_tile(p, cx.tB[:, 0:D], cx.tA[:, 0:D], D, 1e-5)
            p.d("sync", lambda s, tt=tt: [s.dma_start(out=y_d[tt * 128:(tt + 1) * 128, :],
                                                      in_=cx.tB[:, 0:D])], 1)


def build_l0_ffn():
    nc = bass.Bass("TRN2", target_bir_lowering=False, debug=False)
    x_d = nc.dram_tensor("x", [T, D], F32, kind="ExternalInput").ap()
    W = {"norm1_w": nc.dram_tensor("norm1_w", [D], F32, kind="ExternalInput").ap(),
         "norm2_w": nc.dram_tensor("norm2_w", [D], F32, kind="ExternalInput").ap(),
         "qkv_w": nc.dram_tensor("qkv_w", [3 * D, D], F32, kind="ExternalInput").ap(),
         "qkv_b": nc.dram_tensor("qkv_b", [3 * D], F32, kind="ExternalInput").ap(),
         "out_w": nc.dram_tensor("out_w", [D, D], F32, kind="ExternalInput").ap(),
         "out_b": nc.dram_tensor("out_b", [D], F32, kind="ExternalInput").ap(),
         "gate_w": nc.dram_tensor("gate_w", [8, D], F32, kind="ExternalInput").ap(),
         "w1": nc.dram_tensor("w1", [8, 2048, D], F32, kind="ExternalInput").ap(),
         "b1": nc.dram_tensor("b1", [8, 2048], F32, kind="ExternalInput").ap(),
         "w2": nc.dram_tensor("w2", [8, 2048, D], F32, kind="ExternalInput").ap(),
         "b2": nc.dram_tensor("b2", [8, 2048], F32, kind="ExternalInput").ap(),
         "w3": nc.dram_tensor("w3", [8, D, 2048], F32, kind="ExternalInput").ap(),
         "b3": nc.dram_tensor("b3", [8, D], F32, kind="ExternalInput").ap()}
    y_d = nc.dram_tensor("y", [T, D], F32, kind="ExternalOutput").ap()
    with contextlib.ExitStack() as stack:
        cx = Ctx(nc, stack)
        build_program(nc, lambda p: (cx.setup(p), make_l0_ffn_steps(nc, cx, stack, p, x_d, y_d, W)))
    return nc


_cache = {}



def build_full(plan=None):
    nc = bass.Bass("TRN2", target_bir_lowering=False, debug=False)
    x_d = nc.dram_tensor("L0_attn_x", [T, D], F32, kind="ExternalInput").ap()
    y_d = nc.dram_tensor("y", [T, D], F32, kind="ExternalOutput").ap()
    t01 = nc.dram_tensor("t01", [T, D], F32, kind="Internal").ap()
    t12 = nc.dram_tensor("t12", [T, D], F32, kind="Internal").ap()
    t23 = nc.dram_tensor("t23", [T, D], F32, kind="Internal").ap()

    def dt_(n, s):
        return nc.dram_tensor(n, s, F32, kind="ExternalInput").ap()

    W0a = {"ln_g": dt_("L0_attn_ln_g", [D]), "ln_b": dt_("L0_attn_ln_b", [D]),
           "W_in": dt_("L0_attn_W_in", [D, 2 * DI]), "conv_w": dt_("L0_attn_conv_w", [DI, 4]),
           "conv_b": dt_("L0_attn_conv_b", [DI]), "W_x": dt_("L0_attn_W_x", [DI, 2 * DS]),
           "W_dt": dt_("L0_attn_W_dt", [DI, DI]), "b_dt": dt_("L0_attn_b_dt", [DI]),
           "A_log": dt_("L0_attn_A_log", [DS]), "D": dt_("L0_attn_D", [DI]),
           "W_out": dt_("L0_attn_W_out", [DI, D])}
    W0f = {"norm1_w": dt_("L0_ffn_norm1_w", [D]), "norm2_w": dt_("L0_ffn_norm2_w", [D]),
           "qkv_w": dt_("L0_ffn_qkv_w", [3 * D, D]), "qkv_b": dt_("L0_ffn_qkv_b", [3 * D]),
           "out_w": dt_("L0_ffn_out_w", [D, D]), "out_b": dt_("L0_ffn_out_b", [D]),
           "gate_w": dt_("L0_ffn_gate_w", [8, D]), "w1": dt_("L0_ffn_w1", [8, 2048, D]),
           "b1": dt_("L0_ffn_b1", [8, 2048]), "w2": dt_("L0_ffn_w2", [8, 2048, D]),
           "b2": dt_("L0_ffn_b2", [8, 2048]), "w3": dt_("L0_ffn_w3", [8, D, 2048]),
           "b3": dt_("L0_ffn_b3", [8, D])}
    W1a = {"Wq": dt_("L1_attn_Wq", [D, 16, 64]), "Wkv_down": dt_("L1_attn_Wkv_down", [D, 256]),
           "Wk_up": dt_("L1_attn_Wk_up", [256, 16, 64]), "Wv_up": dt_("L1_attn_Wv_up", [256, 16, 64]),
           "Wout": dt_("L1_attn_Wout", [16, 64, D])}
    W1f = {"gate": dt_("L1_ffn_gate", [8, D]), "corr_bias": dt_("L1_ffn_corr_bias", [8]),
           "w1": dt_("L1_ffn_w1", [8, 512, 512]), "w2": dt_("L1_ffn_w2", [8, 512, 512]),
           "fc1_w": dt_("L1_ffn_fc1_w", [512, D]), "fc2_w": dt_("L1_ffn_fc2_w", [D, 512]),
           "up_w": dt_("L1_ffn_up_w", [4096, D]), "down_w": dt_("L1_ffn_down_w", [D, 4096])}

    with contextlib.ExitStack() as stack:
        cx = Ctx(nc, stack)

        def all_steps(p):
            cx.setup(p)
            with contextlib.ExitStack() as s0:
                make_l0_attn_steps(nc, cx, s0, p, x_d, t01, W0a)
            with contextlib.ExitStack() as s1:
                make_l0_ffn_steps(nc, cx, s1, p, t01, t12, W0f)
            with contextlib.ExitStack() as s2:
                make_l1_attn_steps(nc, cx, s2, p, t12, t23, W1a)
            with contextlib.ExitStack() as s3:
                make_l1_ffn_steps(nc, cx, s3, p, t23, y_d, W1f)

        if plan is None:
            meta, caps = build_program(nc, all_steps, record=True)
            return _compute_schedule(meta, caps)
        build_program(nc, all_steps, plan=plan)
    return nc


def build_full_sched():
    plan = build_full(None)
    return build_full(plan)


def _get(name, builder):
    if name not in _cache:
        _cache[name] = builder()
    return _cache[name]


def kernel(**inputs):
    n = 8
    nc = _get("full", build_full_sched)
    maps = []
    for b in range(n):
        m = {k: np.ascontiguousarray(v) for k, v in inputs.items() if k != "L0_attn_x"}
        m["L0_attn_x"] = np.ascontiguousarray(inputs["L0_attn_x"][b])
        maps.append(m)
    res = run_bass_kernel_spmd(nc, maps, core_ids=list(range(n)))
    return np.stack([r["y"] for r in res.results], axis=0)


# version 07 (final)
# speedup vs baseline: 1.2229x; 1.2229x over previous
import numpy as np
import re
import contextlib
import concourse.bass as bass
import concourse.mybir as mybir
from concourse.bass_utils import run_bass_kernel_spmd

F32 = mybir.dt.float32
BF16 = mybir.dt.bfloat16
AF = mybir.ActivationFunctionType
ALU = mybir.AluOpType
AX = mybir.AxisListType

T = 4096
D = 1024
DI = 2048
DS = 64


_OPND = re.compile(r"@([A-Za-z0-9_.\-]+?)(?:\+(\d+))?:\[((?:\[[^\]]*\](?:, )?)+)\]")
_PAIR = re.compile(r"\[(-?\d+), (-?\d+)\]")

NCNT = 6
CC = {"tensor": 0, "vector": 1, "scalar": 2, "gpsimd": 4}
QC = {"sync": 3, "gpsimd": 5}
ENG_NAMES = ("gpsimd", "sync", "tensor", "vector", "scalar")


def _parse_fps(concise, is_dma):
    k = concise.find(" in=[")
    outs_s = concise[: k if k >= 0 else len(concise)]
    ins_s = concise[k:] if k >= 0 else ""

    def parse(region):
        fps = []
        for m in _OPND.finditer(region):
            name = m.group(1)
            off = int(m.group(2) or 0)
            pairs = [(int(a), int(b)) for a, b in _PAIR.findall(m.group(3))]
            if is_dma or len(pairs) < 2 or pairs[0][0] <= 0:
                fps.append((name, None))
                continue
            pitch, nparts = pairs[0]
            plo = off // pitch
            flo = off % pitch
            ext = 1 + sum(abs(st) * (n - 1) for st, n in pairs[1:])
            if flo + ext > pitch:
                flo, fhi = 0, pitch
            else:
                fhi = flo + ext
            fps.append((name, (plo, plo + nparts, flo, fhi)))
        return fps

    return parse(outs_s), parse(ins_s)


def _ivover(a, b):
    if a is None or b is None:
        return True
    return a[0] < b[1] and b[0] < a[1] and a[2] < b[3] and b[2] < a[3]


def _compute_schedule(meta, caps):
    n = len(meta)
    cum = [0] * NCNT
    state = {e: [0] * NCNT for e in ENG_NAMES}
    last_vc_q = {3: [0] * NCNT, 5: [0] * NCNT}
    vc_lookup = [dict() for _ in range(NCNT)]
    acc = {}
    waits = [[] for _ in range(n)]
    MAXL = 24
    for i in range(n):
        eng, nd = meta[i]
        is_dma = nd > 0
        ctr = QC[eng] if is_dma else CC[eng]
        outs = []
        ins = []
        for cs in caps[i]:
            o, ii = _parse_fps(cs, is_dma)
            outs += o
            ins += ii
        need = [0] * NCNT
        for fp in ins:
            rec = acc.get(fp[0])
            if rec:
                for c in range(NCNT):
                    for iv, cv in rec[0][c]:
                        if cv > need[c] and _ivover(fp[1], iv):
                            need[c] = cv
        for fp in outs:
            rec = acc.get(fp[0])
            if rec:
                for c in range(NCNT):
                    for lst in (rec[0][c], rec[1][c]):
                        for iv, cv in lst:
                            if cv > need[c] and _ivover(fp[1], iv):
                                need[c] = cv
        st = state[eng]
        free = {0} if (eng == "tensor" and not is_dma) else set()
        for c in range(NCNT):
            if c in free:
                continue
            if need[c] > st[c]:
                waits[i].append((c, need[c]))
                vcw = vc_lookup[c][need[c]]
                st = [max(a, b) for a, b in zip(st, vcw)]
        cum_after = cum[ctr] + (16 * nd if is_dma else 1)
        cum[ctr] = cum_after
        vc_i = list(st)
        if is_dma:
            vc_i = [max(a, b) for a, b in zip(vc_i, last_vc_q[ctr])]
            vc_i[ctr] = cum_after
            last_vc_q[ctr] = vc_i
        else:
            vc_i[ctr] = cum_after
            if eng == "tensor":
                st = list(st)
                st[ctr] = cum_after
        state[eng] = st
        vc_lookup[ctr][cum_after] = vc_i
        for fps, which in ((outs, 0), (ins, 1)):
            for fp in fps:
                rec = acc.get(fp[0])
                if rec is None:
                    rec = ([[] for _ in range(NCNT)], [[] for _ in range(NCNT)])
                    acc[fp[0]] = rec
                lst = rec[which][ctr]
                lst.append((fp[1], cum_after))
                if len(lst) > MAXL:
                    mx = max(cv for _, cv in lst)
                    del lst[:]
                    lst.append((None, mx))
    return waits, cum


def build_program(nc, make_steps, record=False, plan=None):
    steps = []

    class P:
        def c(self, eng, fn):
            steps.append((eng, (lambda e, f=fn: [f(e)]), 0))

        def g(self, eng, fn):
            steps.append((eng, fn, 0))

        def d(self, eng, fn, n):
            steps.append((eng, fn, n))

    make_steps(P())

    if record:
        caps = [None] * len(steps)
        with nc.Block() as blk:
            def mk(name):
                def body(eng):
                    for idx, (e, fn, nd) in enumerate(steps):
                        if e != name:
                            continue
                        instrs = fn(eng)
                        if nd:
                            assert len(instrs) == nd, (name, len(instrs), nd)
                        caps[idx] = [i.concise() for i in instrs]
                return body

            blk.gpsimd(mk("gpsimd"))
            blk.sync(mk("sync"))
            blk.tensor(mk("tensor"))
            blk.vector(mk("vector"))
            blk.scalar(mk("scalar"))
        return [(e, nd) for e, _, nd in steps], caps

    if plan is not None:
        waits, totals = plan
        assert len(waits) == len(steps)
        with contextlib.ExitStack() as semstack:
            sems = [semstack.enter_context(nc.semaphore(f"dsem{c}")) for c in range(NCNT)]
            with nc.Block() as blk:
                def mk(name):
                    def body(eng):
                        for idx, (e, fn, nd) in enumerate(steps):
                            if e != name:
                                continue
                            for c, v in waits[idx]:
                                eng.wait_ge(sems[c], v)
                            instrs = fn(eng)
                            if nd:
                                assert len(instrs) == nd, (name, len(instrs), nd)
                                for ins in instrs:
                                    ins.then_inc(sems[QC[name]], 16)
                            else:
                                instrs[-1].then_inc(sems[CC[name]], 1)
                        if name == "scalar":
                            for c in range(NCNT):
                                if totals[c] > 0 and c != CC["scalar"]:
                                    eng.wait_ge(sems[c], totals[c])
                    return body

                blk.gpsimd(mk("gpsimd"))
                blk.sync(mk("sync"))
                blk.tensor(mk("tensor"))
                blk.vector(mk("vector"))
                blk.scalar(mk("scalar"))
        return nc

    totals = []
    t = 0
    for eng, fn, nd in steps:
        totals.append(t)
        t += 16 * nd if nd else 1
    final_total = t

    with nc.semaphore("gsem") as sem, nc.Block() as blk:
        def mk(name):
            def body(eng):
                for (e, fn, nd), tb in zip(steps, totals):
                    if e != name:
                        continue
                    if tb > 0:
                        eng.wait_ge(sem, tb)
                    instrs = fn(eng)
                    if nd:
                        assert len(instrs) == nd, (name, len(instrs), nd)
                        for ins in instrs:
                            ins.then_inc(sem, 16)
                    else:
                        instrs[-1].then_inc(sem, 1)
                if name == "scalar":
                    eng.wait_ge(sem, final_total)
            return body

        blk.gpsimd(mk("gpsimd"))
        blk.sync(mk("sync"))
        blk.tensor(mk("tensor"))
        blk.vector(mk("vector"))
        blk.scalar(mk("scalar"))
    return nc


class Ctx:
    def __init__(self, nc, stack):
        self.nc = nc
        sb = lambda n, s, d: stack.enter_context(nc.sbuf_tensor(n, s, d))
        ps = lambda n, s, d: stack.enter_context(nc.psum_tensor(n, s, d))
        self.P0 = ps("P0", [128, 1024], F32)
        self.P1 = ps("P1", [128, 1024], F32)
        self.P2 = ps("P2", [128, 1024], F32)
        self.P3 = ps("P3", [128, 1024], BF16)
        self.P4 = ps("P4", [128, 512], F32)
        self.ident = sb("ident", [128, 128], BF16)
        self.identf = sb("identf", [128, 128], F32)
        self.ones_c128_b = sb("ones_c128_b", [128, 1], BF16)
        self.ones_c64_f = sb("ones_c64_f", [64, 1], F32)
        self.ones_r128_f = sb("ones_r128_f", [1, 128], F32)
        self.tA = sb("tA", [128, 1024], F32)
        self.tB = sb("tB", [128, 1024], F32)
        self.tC = sb("tC", [128, 1024], F32)
        self.tD = sb("tD", [128, 1024], F32)
        self.col = sb("colscr", [128, 8], F32)
        self.kcol = sb("kcol", [128, 4], F32)

    def setup(self, p):
        p.g("gpsimd", lambda g: [
            g.iota(self.tA[:, 0:128], pattern=[[1, 128]], base=0, channel_multiplier=0,
                   allow_small_or_imprecise_dtypes=True),
            g.iota(self.tB[:, 0:128], pattern=[[0, 128]], base=0, channel_multiplier=1,
                   allow_small_or_imprecise_dtypes=True),
            g.memset(self.ones_c128_b[:], 1.0),
            g.memset(self.ones_c64_f[:], 1.0),
            g.memset(self.ones_r128_f[:], 1.0),
            g.memset(self.kcol[:, 0:1], 1.0),
            g.memset(self.kcol[:, 1:2], 1e-5),
            g.memset(self.kcol[:, 2:3], 1e-6),
            g.memset(self.kcol[:, 3:4], 1e-20),
        ])
        p.c("vector", lambda v: v.tensor_tensor(self.identf[:], self.tB[:, 0:128], self.tA[:, 0:128], ALU.is_equal))
        p.c("scalar", lambda s: s.copy(self.ident[:], self.identf[:]))

    def bcast_rows(self, p, dst, src_row, n, npart=128):
        for j0 in range(0, n, 512):
            w = min(512, n - j0)
            p.c("tensor", lambda t, j0=j0, w=w: t.matmul(
                self.P4[0:npart, 0:w], self.ones_r128_f[:, 0:npart],
                src_row[:, j0:j0 + w], start=True, stop=True))
            p.c("scalar", lambda s, j0=j0, w=w: s.copy(dst[:, j0:j0 + w], self.P4[0:npart, 0:w]))

    def transpose_tile(self, p, dst_view, src_bf, ncols):
        nb = ncols // 128
        def mms(t):
            return [t.transpose(self.P3[:, j * 128:(j + 1) * 128],
                                src_bf[:, j * 128:(j + 1) * 128], self.ident[:])
                    for j in range(nb)]
        p.g("tensor", mms)
        p3v = self.P3[:, 0:nb * 128].rearrange("p (n t) -> p n t", n=nb)
        p.c("scalar", lambda s: s.copy(dst_view, p3v))

    def softplus(self, p, out, x_in, bias_col, w):
        if bias_col is None:
            p.c("vector", lambda v: v.tensor_copy(self.tA[:, 0:w], x_in))
        else:
            p.c("vector", lambda v: v.tensor_scalar(self.tA[:, 0:w], x_in, bias_col, None, ALU.add))
        p.c("scalar", lambda s: s.activation(self.tB[:, 0:w], self.tA[:, 0:w], AF.Abs))
        p.c("scalar", lambda s: s.activation(self.tB[:, 0:w], self.tB[:, 0:w], AF.Exp, scale=-1.0))
        p.c("scalar", lambda s: s.activation(self.tB[:, 0:w], self.tB[:, 0:w], AF.Ln, bias=self.kcol[:, 0:1]))
        p.c("vector", lambda v: v.tensor_scalar_max(self.tA[:, 0:w], self.tA[:, 0:w], 0.0))
        p.c("vector", lambda v: v.tensor_add(out, self.tA[:, 0:w], self.tB[:, 0:w]))

    def layernorm_tile(self, p, out, x_f32, w, eps, g_tile=None, b_tile=None, rms=False):
        col = self.col
        mu = col[:, 0:1]; var = col[:, 1:2]; rstd = col[:, 2:3]
        if not rms:
            p.c("vector", lambda v: v.reduce_sum(mu, x_f32, axis=AX.X))
            p.c("scalar", lambda s: s.mul(mu, mu, 1.0 / w))
            p.c("vector", lambda v: v.tensor_scalar(self.tD[:, 0:w], x_f32, mu, None, ALU.subtract))
            src = self.tD[:, 0:w]
        else:
            src = x_f32
        p.c("scalar", lambda s: s.activation(self.tC[:, 0:w], src, AF.Square, accum_out=var))
        ecol = self.kcol[:, 1:2] if eps == 1e-5 else self.kcol[:, 2:3]
        p.c("scalar", lambda s: s.activation(rstd, var, AF.Ln, bias=ecol, scale=1.0 / w))
        p.c("scalar", lambda s: s.activation(rstd, rstd, AF.Exp, scale=-0.5))
        if g_tile is None:
            p.c("vector", lambda v: v.tensor_scalar(out, src, rstd, None, ALU.mult))
        elif b_tile is None:
            p.c("vector", lambda v: v.tensor_scalar(self.tC[:, 0:w], src, rstd, None, ALU.mult))
            p.c("vector", lambda v: v.tensor_mul(out, self.tC[:, 0:w], g_tile))
        else:
            p.c("vector", lambda v: v.tensor_scalar(self.tC[:, 0:w], src, rstd, None, ALU.mult))
            p.c("vector", lambda v: v.tensor_mul(self.tC[:, 0:w], self.tC[:, 0:w], g_tile))
            p.c("vector", lambda v: v.tensor_add(out, self.tC[:, 0:w], b_tile))


def make_l0_attn_steps(nc, cx, stack, p, x_d, y_d, W, prefix="", dbg=None):
    TH = 2048
    sb = lambda n, s, d: stack.enter_context(nc.sbuf_tensor(prefix + n, s, d))
    xnT = sb("xnT", [128, 8 * TH], BF16)
    xcT = sb("xcT", [128, 16 * TH], BF16)

    W_in_r = W["W_in"].rearrange("(n p) o -> p n o", p=128)
    W_dt_r = W["W_dt"].rearrange("(n p) o -> p n o", p=128)
    W_x_r = W["W_x"].rearrange("(n p) o -> p n o", p=128)


    convw = sb("convw", [128, 16, 4], F32)
    convb = sb("convb", [128, 16], F32)
    dcol = sb("dcol", [128, 16], F32)
    bdt_row = sb("bdt_row", [1, 512], BF16)
    ones_r128_b = sb("ones_r128_b", [1, 128], BF16)
    acol = sb("acol", [64, 1], F32)
    xpcarry = sb("xpcarry", [128, 16, 3], BF16)
    hcarry = sb("hcarry", [64, 1], F32)
    g_t = sb("g_t", [128, D], F32)
    b_t = sb("b_t", [128, D], F32)
    wxp = sb("wxp", [128, 16, 128], BF16)
    BstT = sb("BstT", [64, TH], F32)
    CstT = sb("CstT", [64, TH], F32)
    ysB = sb("ysB", [128, TH], BF16)

    p.d("gpsimd", lambda g: [
        g.dma_start(out=convw[:], in_=W["conv_w"].rearrange("(n p) k -> p n k", p=128),
                    allow_slow_non_contiguous=True),
        g.dma_start(out=convb[:], in_=W["conv_b"].rearrange("(n p) -> p n", p=128),
                    allow_slow_non_contiguous=True),
        g.dma_start(out=dcol[:], in_=W["D"].rearrange("(n p) -> p n", p=128),
                    allow_slow_non_contiguous=True),
        g.dma_start(out=acol[:], in_=W["A_log"].rearrange("(p o) -> p o", o=1),
                    allow_slow_non_contiguous=True),
        g.dma_start(out=wxp[:], in_=W_x_r),
    ], 5)
    p.c("scalar", lambda s: s.activation(acol[:], acol[:], AF.Exp))
    p.c("scalar", lambda s: s.mul(acol[:], acol[:], -1.0))
    p.c("gpsimd", lambda g: g.memset(ones_r128_b[:], 1.0))
    p.d("sync", lambda s: [
        s.dma_start(out=cx.tA[0:1, 0:D], in_=W["ln_g"].rearrange("(a b) -> a b", a=1)),
        s.dma_start(out=cx.tB[0:1, 0:D], in_=W["ln_b"].rearrange("(a b) -> a b", a=1))], 2)
    cx.bcast_rows(p, g_t, cx.tA[0:1, 0:D], D)
    cx.bcast_rows(p, b_t, cx.tB[0:1, 0:D], D)

    with contextlib.ExitStack() as ph:
        psb = lambda n, s, d: ph.enter_context(nc.sbuf_tensor(prefix + "ph" + n, s, d))
        xbf = psb("xbf", [128, D], BF16)
        pan = [psb("panelA", [128, 8, 128], BF16), psb("panelB", [128, 8, 128], BF16)]
        panf = [q[:, :, :].rearrange("p a b -> p (a b)") for q in pan]
        ca = psb("ca", [128, TH], F32)
        cb = psb("cb", [128, TH], F32)
        xpp = psb("xpp", [128, 3 + TH], BF16)
        wdtp = psb("wdtp", [128, 16, 512], BF16)
        dmacc = psb("dmacc", [128, 16], F32)
        spf = cx.tC
        drow = psb("drow", [1, TH], F32)
        sres = psb("sres", [128, 512], BF16)
        sres2 = psb("sres2", [128, 512], BF16)
        yT8 = wdtp
        AdT = ca[0:64, :]
        hT = cb[0:64, :]
        ysrow = drow

        for half in range(2):
            t_off = half * TH
            for tt in range(TH // 128):
                p.d("sync", lambda s, tt=tt, t_off=t_off: [s.dma_start(
                    out=cx.tA[:, 0:D], in_=x_d[t_off + tt * 128:t_off + (tt + 1) * 128, :])], 1)
                cx.layernorm_tile(p, xbf[:, 0:D], cx.tA[:, 0:D], D, 1e-5, g_tile=g_t[:, 0:D], b_tile=b_t[:, 0:D])
                dst = xnT[:, :].rearrange("p (n t) -> p n t", n=8)[:, :, tt * 128:(tt + 1) * 128]
                if dbg is not None and half == 0 and tt == 0:
                    p.d("sync", lambda s: [s.dma_start(out=dbg["xn"], in_=xbf[:, 0:D])], 1)
                cx.transpose_tile(p, dst, xbf[:, 0:D], D)

            for ic in range(16):
                h = ic % 2
                p.d("gpsimd", lambda s, ic=ic, h=h: [s.dma_start(
                    out=pan[h][:], in_=W_in_r[:, :, ic * 128:(ic + 1) * 128])], 1)
                for tch in range(TH // 512):
                    pm = cx.P0 if tch % 2 == 0 else cx.P1
                    def mms(t, ic=ic, tch=tch, h=h, pm=pm):
                        return [t.matmul(pm[:, 0:512], pan[h][:, dc],
                                         xnT[:, dc * TH + tch * 512: dc * TH + (tch + 1) * 512],
                                         start=(dc == 0), stop=(dc == 7)) for dc in range(8)]
                    p.g("tensor", mms)
                    p.c("scalar", lambda s, tch=tch, pm=pm: s.copy(
                        xpp[:, 3 + tch * 512:3 + (tch + 1) * 512], pm[:, 0:512]))
                if dbg is not None and half == 0 and ic == 0:
                    p.d("sync", lambda s: [s.dma_start(out=dbg["cw"], in_=convw[:, :, :].rearrange("p a b -> p (a b)")),
                                           s.dma_start(out=dbg["cb"], in_=convb[:]),
                                           s.dma_start(out=dbg["panel"], in_=panf[0]),
                                           s.dma_start(out=dbg["xp"], in_=xpp[:, 3:3 + TH]),
                                           s.dma_start(out=dbg["wxp"], in_=wxp[:, :, :].rearrange("p a b -> p (a b)"))], 5)
                if half == 0:
                    p.c("gpsimd", lambda g: g.memset(xpp[:, 0:3], 0.0))
                else:
                    p.c("vector", lambda v, ic=ic: v.tensor_copy(xpp[:, 0:3], xpcarry[:, ic]))
                p.c("vector", lambda v, ic=ic: v.tensor_copy(xpcarry[:, ic], xpp[:, TH:TH + 3]))
                w0, w1, w2, w3 = [convw[:, ic, k:k + 1] for k in range(4)]
                cbv = convb[:, ic:ic + 1]
                p.c("vector", lambda v, w3=w3, cbv=cbv: v.tensor_scalar(
                    ca[:, 0:TH], xpp[:, 3:3 + TH], w3, cbv, ALU.mult, ALU.add))
                p.c("vector", lambda v, w2=w2: v.scalar_tensor_tensor(
                    cb[:, 0:TH], xpp[:, 2:2 + TH], w2, ca[:, 0:TH], ALU.mult, ALU.add))
                p.c("vector", lambda v, w1=w1: v.scalar_tensor_tensor(
                    ca[:, 0:TH], xpp[:, 1:1 + TH], w1, cb[:, 0:TH], ALU.mult, ALU.add))
                p.c("vector", lambda v, w0=w0: v.scalar_tensor_tensor(
                    cb[:, 0:TH], xpp[:, 0:TH], w0, ca[:, 0:TH], ALU.mult, ALU.add))
                p.c("scalar", lambda s, ic=ic: s.activation(
                    xcT[:, ic * TH:(ic + 1) * TH], cb[:, 0:TH], AF.Silu))
                if dbg is not None and half == 0 and ic == 0:
                    p.d("sync", lambda s: [s.dma_start(out=dbg["xc"], in_=xcT[:, 0:TH]),
                                           s.dma_start(out=dbg["xnT"], in_=xnT[:, 0:TH])], 2)

            for tch in range(TH // 512):
                pm = cx.P0 if tch % 2 == 0 else cx.P1
                def mms(t, tch=tch, pm=pm):
                    return [t.matmul(pm[:, 0:512], wxp[:, icc],
                                     xcT[:, icc * TH + tch * 512:icc * TH + (tch + 1) * 512],
                                     start=(icc == 0), stop=(icc == 15)) for icc in range(16)]
                p.g("tensor", mms)
                p.c("vector", lambda v, tch=tch, pm=pm: v.tensor_copy(
                    BstT[:, tch * 512:(tch + 1) * 512], pm[0:64, 0:512]))
                p.c("vector", lambda v, tch=tch, pm=pm: v.tensor_copy(
                    CstT[:, tch * 512:(tch + 1) * 512], pm[64:128, 0:512]))

            for ocg in range(4):
                p.d("gpsimd", lambda s, ocg=ocg: [s.dma_start(
                    out=wdtp[:], in_=W_dt_r[:, :, ocg * 512:(ocg + 1) * 512]),
                    s.dma_start(out=bdt_row[:], in_=W["b_dt"].rearrange(
                        "(a b) -> a b", a=1)[:, ocg * 512:(ocg + 1) * 512])], 2)
                for tt in range(TH // 128):
                    pm = cx.P0 if tt % 2 == 0 else cx.P1
                    def mms(t, ocg=ocg, tt=tt, pm=pm):
                        out = [t.matmul(pm[:, 0:512],
                                        xcT[:, icc * TH + tt * 128:icc * TH + (tt + 1) * 128],
                                        wdtp[:, icc], start=(icc == 0), stop=False)
                               for icc in range(16)]
                        out.append(t.matmul(pm[:, 0:512], ones_r128_b[:],
                                            bdt_row[:, 0:512],
                                            start=False, stop=True))
                        return out
                    p.g("tensor", mms)
                    cx.softplus(p, spf[:, 0:512], pm[:, 0:512], None, 512)
                    if ocg == 0:
                        p.c("vector", lambda v, tt=tt: v.reduce_sum(
                            dmacc[:, tt:tt + 1], spf[:, 0:512], axis=AX.X))
                    else:
                        p.c("vector", lambda v: v.reduce_sum(cx.col[:, 3:4], spf[:, 0:512], axis=AX.X))
                        p.c("vector", lambda v, tt=tt: v.tensor_add(
                            dmacc[:, tt:tt + 1], dmacc[:, tt:tt + 1], cx.col[:, 3:4]))
            if dbg is not None and half == 0:
                p.d("sync", lambda s: [s.dma_start(out=dbg["B"], in_=BstT[:]),
                                       s.dma_start(out=dbg["C"], in_=CstT[:])], 2)
            p.c("scalar", lambda s: s.mul(dmacc[:], dmacc[:], 1.0 / DI))
            if dbg is not None and half == 0:
                p.d("sync", lambda s: [s.dma_start(out=dbg["dm"], in_=dmacc[:])], 1)
            for tt in range(TH // 128):
                p.c("tensor", lambda t, tt=tt: t.matmul(
                    cx.P4[0:1, 0:128], dmacc[:, tt:tt + 1], cx.identf[:], start=True, stop=True))
                p.c("scalar", lambda s, tt=tt: s.copy(drow[0:1, tt * 128:(tt + 1) * 128], cx.P4[0:1, 0:128]))
            for j0 in range(0, TH, 512):
                p.c("tensor", lambda t, j0=j0: t.matmul(
                    cx.P4[0:64, 0:512], cx.ones_r128_f[:, 0:64],
                    drow[:, j0:j0 + 512], start=True, stop=True))
                p.c("vector", lambda v, j0=j0: v.tensor_scalar(
                    AdT[:, j0:j0 + 512], cx.P4[0:64, 0:512], acol[:], None, ALU.mult))
            p.c("scalar", lambda s: s.activation(AdT[:], AdT[:], AF.Exp))
            if half == 0:
                p.c("vector", lambda v: v.tensor_tensor_scan(
                    hT[:], AdT[:], BstT[:, :], 0.0, ALU.mult, ALU.add))
            else:
                p.c("vector", lambda v: v.tensor_tensor_scan(
                    hT[:], AdT[:], BstT[:, :], hcarry[:], ALU.mult, ALU.add))
            p.c("vector", lambda v: v.tensor_copy(hcarry[:], hT[:, TH - 1:TH]))
            p.c("vector", lambda v: v.tensor_mul(hT[:], hT[:], CstT[:, :]))
            for j0 in range(0, TH, 512):
                p.c("tensor", lambda t, j0=j0: t.matmul(
                    cx.P4[0:1, 0:512], cx.ones_c64_f[:], hT[:, j0:j0 + 512], start=True, stop=True))
                p.c("scalar", lambda s, j0=j0: s.copy(ysrow[0:1, j0:j0 + 512], cx.P4[0:1, 0:512]))
            cx.bcast_rows(p, ysB, ysrow, TH)
            if dbg is not None and half == 0:
                p.d("sync", lambda s: [s.dma_start(out=dbg["ys"], in_=ysB[:])], 1)

            for t8 in range(TH // 512):
                t0 = t8 * 512
                for ic in range(16):
                    h = ic % 2
                    pm2 = cx.P2[:, 0:512] if h == 0 else cx.P2[:, 512:1024]
                    sr = sres if h == 0 else sres2
                    sc5 = cx.tA if h == 0 else cx.tB
                    p.d("gpsimd", lambda s, ic=ic, h=h: [s.dma_start(
                        out=pan[h][:],
                        in_=W_in_r[:, :, 2048 + ic * 128:2048 + (ic + 1) * 128])], 1)
                    def mms(t, ic=ic, h=h, t0=t0, pm2=pm2):
                        return [t.matmul(pm2, pan[h][:, dc],
                                         xnT[:, dc * TH + t0: dc * TH + t0 + 512],
                                         start=(dc == 0), stop=(dc == 7)) for dc in range(8)]
                    p.g("tensor", mms)
                    p.c("scalar", lambda s, sr=sr, pm2=pm2: s.activation(sr[:], pm2, AF.Silu))
                    p.c("vector", lambda v, ic=ic, t0=t0, sc5=sc5: v.scalar_tensor_tensor(
                        sc5[:, 0:512], xcT[:, ic * TH + t0:ic * TH + t0 + 512],
                        dcol[:, ic:ic + 1], ysB[:, t0:t0 + 512], ALU.mult, ALU.add))
                    p.c("vector", lambda v, ic=ic, sr=sr, sc5=sc5: v.tensor_mul(
                        yT8[:, ic], sc5[:, 0:512], sr[:]))
                for rnd in range(2):
                    for ic in range(16):
                        h = ic % 2
                        p.d("gpsimd", lambda s, ic=ic, h=h: [s.dma_start(
                            out=panf[h][:],
                            in_=W["W_out"][ic * 128:(ic + 1) * 128, :])], 1)
                        def mms(t, ic=ic, h=h, rnd=rnd):
                            out = []
                            for tc in range(2):
                                pm = cx.P0 if tc == 0 else cx.P1
                                for ccn in range(2):
                                    out.append(t.matmul(
                                        pm[:, ccn * 512:(ccn + 1) * 512],
                                        yT8[:, ic, (rnd * 2 + tc) * 128:(rnd * 2 + tc + 1) * 128],
                                        panf[h][:, ccn * 512:(ccn + 1) * 512],
                                        start=(ic == 0), stop=(ic == 15)))
                            return out
                        p.g("tensor", mms)
                    for tc in range(2):
                        pm = cx.P0 if tc == 0 else cx.P1
                        trow = t_off + t0 + (rnd * 2 + tc) * 128
                        p.d("sync", lambda s, trow=trow: [s.dma_start(
                            out=cx.tB[:, 0:D], in_=x_d[trow:trow + 128, :])], 1)
                        p.c("vector", lambda v, pm=pm: v.scalar_tensor_tensor(
                            cx.tA[:, 0:D], cx.tB[:, 0:D], 2.0, pm[:, 0:D], ALU.mult, ALU.add))
                        cx.layernorm_tile(p, cx.tB[:, 0:D], cx.tA[:, 0:D], D, 1e-5)
                        p.d("sync", lambda s, trow=trow: [s.dma_start(
                            out=y_d[trow:trow + 128, :], in_=cx.tB[:, 0:D])], 1)


def build_l0_attn(debug=False):
    nc = bass.Bass("TRN2", target_bir_lowering=False, debug=False)
    x_d = nc.dram_tensor("x", [T, D], F32, kind="ExternalInput").ap()
    names = [("ln_g", [D]), ("ln_b", [D]), ("W_in", [D, 2 * DI]), ("conv_w", [DI, 4]),
             ("conv_b", [DI]), ("W_x", [DI, 2 * DS]), ("W_dt", [DI, DI]), ("b_dt", [DI]),
             ("A_log", [DS]), ("D", [DI]), ("W_out", [DI, D])]
    W = {n: nc.dram_tensor(n, s, F32, kind="ExternalInput").ap() for n, s in names}
    y_d = nc.dram_tensor("y", [T, D], F32, kind="ExternalOutput").ap()
    dbg = None
    if debug:
        dbg = {"xn": nc.dram_tensor("dbg_xn", [128, D], BF16, kind="ExternalOutput").ap(),
               "xc": nc.dram_tensor("dbg_xc", [128, 2048], BF16, kind="ExternalOutput").ap(),
               "xnT": nc.dram_tensor("dbg_xnT", [128, 2048], BF16, kind="ExternalOutput").ap(),
               "B": nc.dram_tensor("dbg_B", [64, 2048], F32, kind="ExternalOutput").ap(),
               "C": nc.dram_tensor("dbg_C", [64, 2048], F32, kind="ExternalOutput").ap(),
               "dm": nc.dram_tensor("dbg_dm", [128, 16], F32, kind="ExternalOutput").ap(),
               "ys": nc.dram_tensor("dbg_ys", [128, 2048], BF16, kind="ExternalOutput").ap(),
               "panel": nc.dram_tensor("dbg_panel", [128, 1024], BF16, kind="ExternalOutput").ap(),
               "xp": nc.dram_tensor("dbg_xp", [128, 2048], BF16, kind="ExternalOutput").ap(),
               "wxp": nc.dram_tensor("dbg_wxp", [128, 2048], BF16, kind="ExternalOutput").ap(),
               "cw": nc.dram_tensor("dbg_cw", [128, 64], F32, kind="ExternalOutput").ap(),
               "cb": nc.dram_tensor("dbg_cb", [128, 16], F32, kind="ExternalOutput").ap()}
    with contextlib.ExitStack() as stack:
        cx = Ctx(nc, stack)
        build_program(nc, lambda p: (cx.setup(p), make_l0_attn_steps(nc, cx, stack, p, x_d, y_d, W, dbg=dbg)))
    return nc




def add_ctx_attn(nc, cx, stack, p, pfx=""):
    sb = lambda n, s, d: stack.enter_context(nc.sbuf_tensor(pfx + n, s, d))
    cx.ptA = sb("ptA", [128, 512], BF16)
    cx.ptB = sb("ptB", [128, 512], BF16)
    cx.ptA2 = sb("ptA2", [128, 512], BF16)
    cx.ptB2 = sb("ptB2", [128, 512], BF16)
    cx.dnrow = sb("dnrow", [1, 512], F32)
    cx.dnb = sb("dnb", [64, 512], F32)
    cx.stgA = sb("stgA", [64, 512], BF16)
    cx.stgB = sb("stgB", [64, 512], BF16)


def load_transposed(nc, cx, p, dst, w_dram, out_dim, in_dim, stages):
    nic = in_dim // 128
    for oc in range(out_dim // 128):
        stage = stages[oc % 2]
        p.d("gpsimd", lambda g, oc=oc, stage=stage: [g.dma_start(
            out=stage[:, 0:in_dim], in_=w_dram[oc * 128:(oc + 1) * 128, :])], 1)
        for icg in range(0, nic, 8):
            nb = min(8, nic - icg)
            def mms(t, icg=icg, nb=nb, stage=stage):
                return [t.transpose(cx.P3[:, j * 128:(j + 1) * 128],
                                    stage[:, (icg + j) * 128:(icg + j + 1) * 128], cx.ident[:])
                        for j in range(nb)]
            p.g("tensor", mms)
            dv = dst[:, icg:icg + nb, oc * 128:(oc + 1) * 128]
            p3v = cx.P3[:, 0:nb * 128].rearrange("p (n t) -> p n t", n=nb)
            p.c("scalar", lambda s, dv=dv, p3v=p3v: s.copy(dv, p3v))


def attn_one_hp(nc, cx, p, hp, qT_hp, kT_hp, vaug, attn_dram, T_, causal, masks):
    dnrow, dnb = cx.dnrow, cx.dnb
    stgA, stgB = cx.stgA, cx.stgB
    NQ = T_ // 512
    KCALL = T_ // 128
    for qb in range(NQ):
        kmax = min(KCALL, 4 * (qb + 1)) if causal else KCALL
        for kc in range(kmax):
            pm = cx.P0 if kc % 2 == 0 else cx.P1
            pa = cx.ptA if kc % 2 == 0 else cx.ptA2
            pb = cx.ptB if kc % 2 == 0 else cx.ptB2
            def smm(t, qb=qb, kc=kc, pm=pm):
                return [
                    t.matmul(pm[:, 0:512], kT_hp[0:64, kc * 128:(kc + 1) * 128],
                             qT_hp[0:64, qb * 512:(qb + 1) * 512], start=True, stop=True),
                    t.matmul(pm[:, 512:1024], kT_hp[64:128, kc * 128:(kc + 1) * 128],
                             qT_hp[64:128, qb * 512:(qb + 1) * 512], start=True, stop=True)]
            p.g("tensor", smm)
            p.c("scalar", lambda s, pa=pa, pm=pm: s.activation(pa[:], pm[:, 0:512], AF.Exp, scale=0.125))
            p.c("scalar", lambda s, pb=pb, pm=pm: s.activation(pb[:], pm[:, 512:1024], AF.Exp, scale=0.125))
            if causal and kc >= 4 * qb:
                o = (kc - 4 * qb)
                p.c("vector", lambda v, o=o, pa=pa: v.tensor_mul(pa[:], pa[:], masks[o][:]))
                p.c("vector", lambda v, o=o, pb=pb: v.tensor_mul(pb[:], pb[:], masks[o][:]))
            def pvmm(t, qb=qb, kc=kc, kmax=kmax, pa=pa, pb=pb):
                return [
                    t.matmul(cx.P2[0:65, 0:512], vaug[:, 0, kc, :], pa[:],
                             start=(kc == 0), stop=(kc == kmax - 1)),
                    t.matmul(cx.P2[0:65, 512:1024], vaug[:, 1, kc, :], pb[:],
                             start=(kc == 0), stop=(kc == kmax - 1))]
            p.g("tensor", pvmm)
        for h, off in ((0, 0), (1, 512)):
            stg = stgA if h == 0 else stgB
            p.c("vector", lambda v, off=off: v.reciprocal(dnrow[:], cx.P2[64:65, off:off + 512]))
            p.c("tensor", lambda t: t.matmul(cx.P4[0:64, 0:512], cx.ones_r128_f[:, 0:64],
                                             dnrow[:], start=True, stop=True))
            p.c("scalar", lambda s: s.copy(dnb[:], cx.P4[0:64, 0:512]))
            p.c("vector", lambda v, off=off, stg=stg: v.tensor_mul(
                stg[:], cx.P2[0:64, off:off + 512], dnb[:]))
            p.d("sync", lambda s, h=h, stg=stg, qb=qb, hp=hp: [s.dma_start(
                out=attn_dram[hp * 128 + h * 64:hp * 128 + h * 64 + 64,
                              qb * 512:(qb + 1) * 512], in_=stg[:])], 1)


def build_causal_masks(nc, cx, stack, p):
    sb = lambda n, s, d: stack.enter_context(nc.sbuf_tensor(n, s, d))
    masks = [sb(f"mask{o}", [128, 512], BF16) for o in range(4)]
    cj = sb("mcj", [128, 512], F32)
    rp = sb("mrp", [128, 512], F32)
    p.g("gpsimd", lambda g: [
        g.iota(cj[:], pattern=[[1, 512]], base=0, channel_multiplier=0,
               allow_small_or_imprecise_dtypes=True),
        g.iota(rp[:], pattern=[[0, 512]], base=0, channel_multiplier=1,
               allow_small_or_imprecise_dtypes=True)])
    for o in range(4):
        p.c("vector", lambda v, o=o: v.tensor_single_scalar(cx.tA[:, 0:512], rp[:], float(o * 128), ALU.add))
        p.c("vector", lambda v, o=o: v.tensor_tensor(masks[o][:], cx.tA[:, 0:512], cj[:], ALU.is_le))
    return masks


def make_l1_attn_steps(nc, cx, stack, p, x_d, y_d, W):
    sb = lambda n, s, d: stack.enter_context(nc.sbuf_tensor(n, s, d))
    add_ctx_attn(nc, cx, stack, p)
    masks = build_causal_masks(nc, cx, stack, p)
    attn_dram = nc.dram_tensor("attn_scr", [1024, T], BF16, kind="Internal").ap()

    Wq_r = W["Wq"].rearrange("(n p) h o -> p n (h o)", p=128)
    Wkvd_r = W["Wkv_down"].rearrange("(n p) o -> p n o", p=128)
    Wku_r = W["Wk_up"].rearrange("(n p) h o -> p n (h o)", p=128)
    Wvu_r = W["Wv_up"].rearrange("(n p) h o -> p n (h o)", p=128)
    Wout_r = W["Wout"].rearrange("(n h2) o d -> (h2 o) n d", h2=2)

    wq = sb("wq", [128, 8, 1024], BF16)
    wkvd = sb("wkvd", [128, 8, 256], BF16)
    wku = sb("wku", [128, 2, 1024], BF16)
    wvu = sb("wvu", [128, 2, 1024], BF16)
    wout = sb("wout", [128, 8, 1024], BF16)
    latT = sb("latT", [128, 2 * T], BF16)
    qT = sb("qTl1", [128, T], BF16)
    kT = sb("kTl1", [128, T], BF16)
    vaug = sb("vaugl1", [128, 2, T // 128, 65], BF16)
    xbf = sb("xbfl1", [128, D], BF16)

    p.d("gpsimd", lambda g: [
        g.dma_start(out=wq[:], in_=Wq_r),
        g.dma_start(out=wkvd[:], in_=Wkvd_r),
        g.dma_start(out=wku[:], in_=Wku_r),
        g.dma_start(out=wvu[:], in_=Wvu_r),
        g.dma_start(out=wout[:], in_=Wout_r),
    ], 5)

    with contextlib.ExitStack() as ph:
        psb = lambda n, s, d: ph.enter_context(nc.sbuf_tensor("l1a" + n, s, d))
        xT = psb("xT", [128, 8 * T], BF16)
        for tt in range(T // 128):
            p.d("sync", lambda s, tt=tt: [s.dma_start(out=cx.tA[:, 0:D],
                                                      in_=x_d[tt * 128:(tt + 1) * 128, :])], 1)
            p.c("scalar", lambda s: s.copy(xbf[:, 0:D], cx.tA[:, 0:D]))
            dst = xT[:, :].rearrange("p (n t) -> p n t", n=8)[:, :, tt * 128:(tt + 1) * 128]
            cx.transpose_tile(p, dst, xbf[:, 0:D], D)
        for lc in range(2):
            for tch in range(T // 512):
                pm = cx.P0 if tch % 2 == 0 else cx.P1
                def mms(t, lc=lc, tch=tch, pm=pm):
                    return [t.matmul(pm[:, 0:512], wkvd[:, dc, lc * 128:(lc + 1) * 128],
                                     xT[:, dc * T + tch * 512:dc * T + (tch + 1) * 512],
                                     start=(dc == 0), stop=(dc == 7)) for dc in range(8)]
                p.g("tensor", mms)
                p.c("scalar", lambda s, lc=lc, tch=tch, pm=pm: s.copy(
                    latT[:, lc * T + tch * 512:lc * T + (tch + 1) * 512], pm[:, 0:512]))
        for hp in range(8):
            for tch in range(T // 512):
                pm = cx.P0 if tch % 2 == 0 else cx.P1
                def mms(t, hp=hp, tch=tch, pm=pm):
                    return [t.matmul(pm[:, 0:512], wq[:, dc, hp * 128:(hp + 1) * 128],
                                     xT[:, dc * T + tch * 512:dc * T + (tch + 1) * 512],
                                     start=(dc == 0), stop=(dc == 7)) for dc in range(8)]
                p.g("tensor", mms)
                p.c("scalar", lambda s, tch=tch, pm=pm: s.copy(
                    qT[:, tch * 512:(tch + 1) * 512], pm[:, 0:512]))
                def mmsk(t, hp=hp, tch=tch, pm=pm):
                    return [t.matmul(pm[:, 0:512], wku[:, lc, hp * 128:(hp + 1) * 128],
                                     latT[:, lc * T + tch * 512:lc * T + (tch + 1) * 512],
                                     start=(lc == 0), stop=(lc == 1)) for lc in range(2)]
                p.g("tensor", mmsk)
                p.c("scalar", lambda s, tch=tch, pm=pm: s.copy(
                    kT[:, tch * 512:(tch + 1) * 512], pm[:, 0:512]))
            p.c("gpsimd", lambda g: g.memset(vaug[:, :, :, 64:65], 1.0))
            for tc in range(T // 128):
                def mmsv(t, hp=hp, tc=tc):
                    return [t.matmul(cx.P2[:, 0:128], latT[:, lc * T + tc * 128:lc * T + (tc + 1) * 128],
                                     wvu[:, lc, hp * 128:(hp + 1) * 128],
                                     start=(lc == 0), stop=(lc == 1)) for lc in range(2)]
                p.g("tensor", mmsv)
                p.c("scalar", lambda s, tc=tc: s.copy(vaug[:, 0, tc, 0:64], cx.P2[:, 0:64]))
                p.c("scalar", lambda s, tc=tc: s.copy(vaug[:, 1, tc, 0:64], cx.P2[:, 64:128]))
            attn_one_hp(nc, cx, p, hp, qT, kT, vaug, attn_dram, T, True, masks)

    with contextlib.ExitStack() as ph:
        psb = lambda n, s, d: ph.enter_context(nc.sbuf_tensor("l1o" + n, s, d))
        atile = psb("atile", [128, 8, 128], BF16)
        for tc in range(T // 128):
            p.d("sync", lambda s, tc=tc: [s.dma_start(
                out=atile[:],
                in_=attn_dram.rearrange("(n p) t -> p n t", p=128)[:, :, tc * 128:(tc + 1) * 128])], 1)
            def mms(t, tc=tc):
                out = []
                for hpc in range(8):
                    for cc in range(2):
                        out.append(t.matmul(cx.P1[:, cc * 512:(cc + 1) * 512],
                                            atile[:, hpc, :],
                                            wout[:, hpc, cc * 512:(cc + 1) * 512],
                                            start=(hpc == 0), stop=(hpc == 7)))
                return out
            p.g("tensor", mms)
            p.d("sync", lambda s, tc=tc: [s.dma_start(
                out=cx.tB[:, 0:D], in_=x_d[tc * 128:(tc + 1) * 128, :])], 1)
            p.c("vector", lambda v: v.tensor_add(cx.tA[:, 0:D], cx.tB[:, 0:D], cx.P1[:, 0:D]))
            cx.layernorm_tile(p, cx.tB[:, 0:D], cx.tA[:, 0:D], D, 1e-5)
            p.d("sync", lambda s, tc=tc: [s.dma_start(
                out=y_d[tc * 128:(tc + 1) * 128, :], in_=cx.tB[:, 0:D])], 1)


def build_l1_attn():
    nc = bass.Bass("TRN2", target_bir_lowering=False, debug=False)
    x_d = nc.dram_tensor("x", [T, D], F32, kind="ExternalInput").ap()
    W = {"Wq": nc.dram_tensor("Wq", [D, 16, 64], F32, kind="ExternalInput").ap(),
         "Wkv_down": nc.dram_tensor("Wkv_down", [D, 256], F32, kind="ExternalInput").ap(),
         "Wk_up": nc.dram_tensor("Wk_up", [256, 16, 64], F32, kind="ExternalInput").ap(),
         "Wv_up": nc.dram_tensor("Wv_up", [256, 16, 64], F32, kind="ExternalInput").ap(),
         "Wout": nc.dram_tensor("Wout", [16, 64, D], F32, kind="ExternalInput").ap()}
    y_d = nc.dram_tensor("y", [T, D], F32, kind="ExternalOutput").ap()
    with contextlib.ExitStack() as stack:
        cx = Ctx(nc, stack)
        build_program(nc, lambda p: (cx.setup(p), make_l1_attn_steps(nc, cx, stack, p, x_d, y_d, W)))
    return nc


def make_l1_ffn_steps(nc, cx, stack, p, x_d, y_d, W):
    sb = lambda n, s, d: stack.enter_context(nc.sbuf_tensor(n, s, d))
    E = 8; LAT = 512; INTER = 512; SH = 4096
    routed_dram = nc.dram_tensor("routed_scr", [T, D], F32, kind="Internal").ap()
    upT_dram = nc.dram_tensor("upT_scr", [D, SH], BF16, kind="Internal").ap()

    xfT = sb("xfT", [128, 8 * T], BF16)
    wtab = sb("wtab", [128, T // 128, 8], F32)
    comb = sb("comb", [128, T // 128, 512], BF16)
    xbf = sb("xbff", [128, D], BF16)
    stage = sb("stagew", [128, 4096], BF16)
    stageB = sb("stagewB", [128, 1024], BF16)
    stg2 = [stage, stageB]
    cbB = sb("cbB", [128, 8], F32)
    ip8 = sb("ip8", [128, 8], F32)
    ones8 = sb("ones8", [128, 8], F32)
    gateT = sb("gateT", [128, 8, 8], BF16)
    sc8 = sb("sc8", [128, 8], F32)
    sf8 = sb("sf8", [128, 8], F32)
    e8 = sb("e8", [128, 8], F32)

    for tt in range(T // 128):
        p.d("sync", lambda s, tt=tt: [s.dma_start(out=cx.tA[:, 0:D],
                                                  in_=x_d[tt * 128:(tt + 1) * 128, :])], 1)
        p.c("scalar", lambda s: s.copy(xbf[:, 0:D], cx.tA[:, 0:D]))
        dst = xfT[:, :].rearrange("p (n t) -> p n t", n=8)[:, :, tt * 128:(tt + 1) * 128]
        cx.transpose_tile(p, dst, xbf[:, 0:D], D)

    p.d("gpsimd", lambda g: [g.dma_start(out=stage[0:8, 0:1024], in_=W["gate"][:, :]),
                             g.dma_start(out=cx.tD[0:1, 0:8], in_=W["corr_bias"].rearrange("(a b) -> a b", a=1))], 2)
    for dc in range(8):
        p.c("tensor", lambda t, dc=dc: t.transpose(cx.P3[:, 0:8], stage[0:8, dc * 128:(dc + 1) * 128], cx.ident[0:8, 0:8]))
        p.c("scalar", lambda s, dc=dc: s.copy(gateT[:, dc, :], cx.P3[:, 0:8]))
    cx.bcast_rows(p, cbB, cx.tD[0:1, 0:8], 8)
    p.g("gpsimd", lambda g: [g.iota(ip8[:], pattern=[[1, 8]], base=0, channel_multiplier=0,
                                    allow_small_or_imprecise_dtypes=True),
                             g.memset(ones8[:], 1.0)])
    p.c("scalar", lambda s: s.mul(ip8[:], ip8[:], 1e-18))
    t8a = cx.col[:, 0:1]; t8b = cx.col[:, 1:2]; t8c = cx.col[:, 2:3]; t8d = cx.col[:, 3:4]
    for tt in range(T // 128):
        def mms(t, tt=tt):
            return [t.matmul(cx.P4[:, 0:8], xfT[:, dc * T + tt * 128:dc * T + (tt + 1) * 128],
                             gateT[:, dc, :], start=(dc == 0), stop=(dc == 7)) for dc in range(8)]
        p.g("tensor", mms)
        p.c("scalar", lambda s: s.activation(sc8[:], cx.P4[:, 0:8], AF.Sigmoid))
        p.c("vector", lambda v: v.tensor_add(sf8[:], sc8[:], cbB[:]))
        for gi, sl in ((0, slice(0, 4)), (1, slice(4, 8))):
            m1 = cx.col[:, 4 + gi:5 + gi]
            p.c("vector", lambda v, sl=sl, m1=m1: v.reduce_max(m1, sf8[:, sl], axis=AX.X))
            p.c("vector", lambda v, sl=sl, m1=m1: v.tensor_scalar(e8[:, sl], sf8[:, sl], m1, None, ALU.is_ge))
            p.c("vector", lambda v, sl=sl: v.scalar_tensor_tensor(e8[:, sl], e8[:, sl], -1e30, sf8[:, sl], ALU.mult, ALU.add))
            m2 = t8a if gi == 0 else t8b
            p.c("vector", lambda v, sl=sl, m2=m2: v.reduce_max(m2, e8[:, sl], axis=AX.X))
        gs0 = cx.col[:, 6:7]; gs1 = cx.col[:, 7:8]
        p.c("vector", lambda v: v.tensor_add(gs0, cx.col[:, 4:5], t8a))
        p.c("vector", lambda v: v.tensor_add(gs1, cx.col[:, 5:6], t8b))
        p.c("vector", lambda v: v.tensor_tensor(t8c, gs0, gs1, ALU.is_ge))
        p.c("vector", lambda v: v.tensor_scalar(t8d, t8c, -1.0, 1.0, ALU.mult, ALU.add))
        p.c("vector", lambda v: v.tensor_scalar(e8[:, 0:4], ones8[:, 0:4], t8c, None, ALU.mult))
        p.c("vector", lambda v: v.tensor_scalar(e8[:, 4:8], ones8[:, 4:8], t8d, None, ALU.mult))
        p.c("vector", lambda v: v.tensor_mul(sf8[:], sf8[:], e8[:]))
        p.c("vector", lambda v: v.tensor_sub(sf8[:], sf8[:], ip8[:]))
        p.c("vector", lambda v: v.reduce_max(t8a, sf8[:], axis=AX.X))
        p.c("vector", lambda v: v.tensor_scalar(e8[:], sf8[:], t8a, None, ALU.is_ge))
        p.c("vector", lambda v: v.scalar_tensor_tensor(e8[:], e8[:], -1e30, sf8[:], ALU.mult, ALU.add))
        p.c("vector", lambda v: v.reduce_max(t8b, e8[:], axis=AX.X))
        p.c("vector", lambda v: v.tensor_scalar(e8[:], sf8[:], t8b, None, ALU.is_ge))
        p.c("vector", lambda v: v.tensor_mul(e8[:], e8[:], sc8[:]))
        p.c("vector", lambda v: v.reduce_sum(t8c, e8[:], axis=AX.X))
        p.c("vector", lambda v: v.tensor_scalar(t8c, t8c, cx.kcol[:, 3:4], None, ALU.add))
        p.c("vector", lambda v: v.reciprocal(t8d, t8c))
        p.c("vector", lambda v, tt=tt: v.tensor_scalar(wtab[:, tt, :], e8[:], t8d, None, ALU.mult))

    with contextlib.ExitStack() as ph:
        psb = lambda n, s, d: ph.enter_context(nc.sbuf_tensor("l1f" + n, s, d))
        fc1T = psb("fc1T", [128, 8, 512], BF16)
        latT = psb("latT", [128, 4 * T], BF16)
        h1T = psb("h1T", [128, 4 * T], BF16)
        w1eT = psb("w1eT", [128, 4, 512], BF16)
        w2eT = psb("w2eT", [128, 4, 512], BF16)
        load_transposed(nc, cx, p, fc1T, W["fc1_w"], 512, 1024, stg2)
        for Lc in range(4):
            for tch in range(T // 512):
                pm = cx.P0 if tch % 2 == 0 else cx.P1
                def mms(t, Lc=Lc, tch=tch, pm=pm):
                    return [t.matmul(pm[:, 0:512], fc1T[:, dc, Lc * 128:(Lc + 1) * 128],
                                     xfT[:, dc * T + tch * 512:dc * T + (tch + 1) * 512],
                                     start=(dc == 0), stop=(dc == 7)) for dc in range(8)]
                p.g("tensor", mms)
                p.c("scalar", lambda s, Lc=Lc, tch=tch, pm=pm: s.copy(
                    latT[:, Lc * T + tch * 512:Lc * T + (tch + 1) * 512], pm[:, 0:512]))
        p.c("vector", lambda v: v.memset(comb[:], 0.0))
        for e in range(E):
            load_transposed(nc, cx, p, w1eT, W["w1"][e], INTER, LAT, stg2)
            load_transposed(nc, cx, p, w2eT, W["w2"][e], LAT, INTER, stg2)
            for icc in range(4):
                for tch in range(T // 512):
                    pm = cx.P0 if tch % 2 == 0 else cx.P1
                    def mms(t, icc=icc, tch=tch, pm=pm):
                        return [t.matmul(pm[:, 0:512], w1eT[:, lc, icc * 128:(icc + 1) * 128],
                                         latT[:, lc * T + tch * 512:lc * T + (tch + 1) * 512],
                                         start=(lc == 0), stop=(lc == 3)) for lc in range(4)]
                    p.g("tensor", mms)
                    p.c("vector", lambda v, pm=pm: v.tensor_scalar_max(cx.tA[:, 0:512], pm[:, 0:512], 0.0))
                    p.c("scalar", lambda s, icc=icc, tch=tch: s.activation(
                        h1T[:, icc * T + tch * 512:icc * T + (tch + 1) * 512],
                        cx.tA[:, 0:512], AF.Square))
            for tt in range(T // 128):
                def mms(t, tt=tt):
                    return [t.matmul(cx.P2[:, 0:512], h1T[:, icc * T + tt * 128:icc * T + (tt + 1) * 128],
                                     w2eT[:, icc, :], start=(icc == 0), stop=(icc == 3))
                            for icc in range(4)]
                p.g("tensor", mms)
                p.c("vector", lambda v, tt=tt, e=e: v.scalar_tensor_tensor(
                    comb[:, tt, :], cx.P2[:, 0:512], wtab[:, tt, e:e + 1],
                    comb[:, tt, :], ALU.mult, ALU.add))

    with contextlib.ExitStack() as ph:
        psb = lambda n, s, d: ph.enter_context(nc.sbuf_tensor("l1g" + n, s, d))
        fc2T = psb("fc2T", [128, 4, 1024], BF16)
        ctile = psb("ctile", [128, 4, 128], BF16)
        load_transposed(nc, cx, p, fc2T, W["fc2_w"], 1024, 512, stg2)
        for tt in range(T // 128):
            def tmms(t, tt=tt):
                return [t.transpose(cx.P3[:, j * 128:(j + 1) * 128],
                                    comb[:, tt, j * 128:(j + 1) * 128], cx.ident[:]) for j in range(4)]
            p.g("tensor", tmms)
            p.c("scalar", lambda s: s.copy(ctile[:], cx.P3[:, 0:512].rearrange("p (n t) -> p n t", n=4)))
            def mms(t, tt=tt):
                out = []
                for lc in range(4):
                    for cc in range(2):
                        out.append(t.matmul(cx.P1[:, cc * 512:(cc + 1) * 512], ctile[:, lc, :],
                                            fc2T[:, lc, cc * 512:(cc + 1) * 512],
                                            start=(lc == 0), stop=(lc == 3)))
                return out
            p.g("tensor", mms)
            p.c("vector", lambda v: v.tensor_copy(cx.tC[:, 0:D], cx.P1[:, 0:D]))
            p.d("sync", lambda s, tt=tt: [s.dma_start(out=routed_dram[tt * 128:(tt + 1) * 128, :],
                                                      in_=cx.tC[:, 0:D])], 1)

    with contextlib.ExitStack() as ph:
        psb = lambda n, s, d: ph.enter_context(nc.sbuf_tensor("l1h" + n, s, d))
        dwT = psb("dwT", [128, 32, 1024], BF16)
        upc = psb("upc", [128, 8, 128], BF16)
        upc2 = psb("upc2", [128, 8, 128], BF16)
        stc = psb("stc", [128, 2, 256], BF16)
        upT_r = upT_dram.rearrange("(n p) s -> p n s", p=128)
        for sc in range(32):
            stg = stg2[sc % 2]
            p.d("gpsimd", lambda g, sc=sc, stg=stg: [g.dma_start(
                out=stg[:, 0:1024], in_=W["up_w"][sc * 128:(sc + 1) * 128, :])], 1)
            def tmms(t, stg=stg):
                return [t.transpose(cx.P3[:, j * 128:(j + 1) * 128],
                                    stg[:, j * 128:(j + 1) * 128], cx.ident[:]) for j in range(8)]
            p.g("tensor", tmms)
            p.c("scalar", lambda s: s.copy(upc[:], cx.P3[:, 0:1024].rearrange("p (n t) -> p n t", n=8)))
            p.d("sync", lambda s, sc=sc: [s.dma_start(out=upT_r[:, :, sc * 128:(sc + 1) * 128],
                                                      in_=upc[:])], 1)
        load_transposed(nc, cx, p, dwT, W["down_w"], 1024, 4096, [stage, stage])
        for tg in range(T // 256):
            for sc in range(32):
                h = sc % 2
                uc = upc if h == 0 else upc2
                pm2 = cx.P2[:, 0:256] if h == 0 else cx.P2[:, 256:512]
                sc6 = cx.tA if h == 0 else cx.tB
                p.d("gpsimd", lambda g, sc=sc, uc=uc: [g.dma_start(
                    out=uc[:], in_=upT_r[:, :, sc * 128:(sc + 1) * 128])], 1)
                def mms(t, tg=tg, sc=sc, uc=uc, pm2=pm2):
                    return [t.matmul(pm2, uc[:, dc, :],
                                     xfT[:, dc * T + tg * 256:dc * T + (tg + 1) * 256],
                                     start=(dc == 0), stop=(dc == 7)) for dc in range(8)]
                p.g("tensor", mms)
                p.c("vector", lambda v, pm2=pm2, sc6=sc6: v.tensor_scalar_max(sc6[:, 0:256], pm2, 0.0))
                p.c("scalar", lambda s, h=h, sc6=sc6: s.activation(stc[:, h, :], sc6[:, 0:256], AF.Square))
                def dmms(t, sc=sc, h=h):
                    out = []
                    for tc in range(2):
                        pm = cx.P0 if tc == 0 else cx.P1
                        for cc in range(2):
                            out.append(t.matmul(pm[:, cc * 512:(cc + 1) * 512],
                                                stc[:, h, tc * 128:(tc + 1) * 128],
                                                dwT[:, sc, cc * 512:(cc + 1) * 512],
                                                start=(sc == 0), stop=(sc == 31)))
                    return out
                p.g("tensor", dmms)
            for tc in range(2):
                pm = cx.P0 if tc == 0 else cx.P1
                trow = tg * 256 + tc * 128
                p.d("sync", lambda s, trow=trow: [
                    s.dma_start(out=cx.tB[:, 0:D], in_=x_d[trow:trow + 128, :]),
                    s.dma_start(out=cx.tD[:, 0:D], in_=routed_dram[trow:trow + 128, :])], 2)
                p.c("vector", lambda v, pm=pm: v.tensor_add(cx.tA[:, 0:D], cx.tB[:, 0:D], pm[:, 0:D]))
                p.c("vector", lambda v: v.tensor_add(cx.tA[:, 0:D], cx.tA[:, 0:D], cx.tD[:, 0:D]))
                cx.layernorm_tile(p, cx.tB[:, 0:D], cx.tA[:, 0:D], D, 1e-5)
                p.d("sync", lambda s, trow=trow: [s.dma_start(
                    out=y_d[trow:trow + 128, :], in_=cx.tB[:, 0:D])], 1)


def build_l1_ffn():
    nc = bass.Bass("TRN2", target_bir_lowering=False, debug=False)
    x_d = nc.dram_tensor("x", [T, D], F32, kind="ExternalInput").ap()
    W = {"gate": nc.dram_tensor("gate", [8, D], F32, kind="ExternalInput").ap(),
         "corr_bias": nc.dram_tensor("corr_bias", [8], F32, kind="ExternalInput").ap(),
         "w1": nc.dram_tensor("w1", [8, 512, 512], F32, kind="ExternalInput").ap(),
         "w2": nc.dram_tensor("w2", [8, 512, 512], F32, kind="ExternalInput").ap(),
         "fc1_w": nc.dram_tensor("fc1_w", [512, D], F32, kind="ExternalInput").ap(),
         "fc2_w": nc.dram_tensor("fc2_w", [D, 512], F32, kind="ExternalInput").ap(),
         "up_w": nc.dram_tensor("up_w", [4096, D], F32, kind="ExternalInput").ap(),
         "down_w": nc.dram_tensor("down_w", [D, 4096], F32, kind="ExternalInput").ap()}
    y_d = nc.dram_tensor("y", [T, D], F32, kind="ExternalOutput").ap()
    with contextlib.ExitStack() as stack:
        cx = Ctx(nc, stack)
        build_program(nc, lambda p: (cx.setup(p), make_l1_ffn_steps(nc, cx, stack, p, x_d, y_d, W)))
    return nc


def make_l0_ffn_steps(nc, cx, stack, p, x_d, y_d, W):
    sb = lambda n, s, d: stack.enter_context(nc.sbuf_tensor(n, s, d))
    attn_dram = nc.dram_tensor("attn_scr0", [1024, T], BF16, kind="Internal").ap()
    x1_dram = nc.dram_tensor("x1_scr", [T, D], F32, kind="Internal").ap()
    acc_dram = nc.dram_tensor("acc_scr", [T, D], F32, kind="Internal").ap()

    wtab = sb("wtab0", [128, T // 128, 8], F32)
    xbf = sb("xbf0", [128, D], BF16)
    stage = sb("stagew0", [128, 2048], BF16)
    stageB = sb("stagew0B", [128, 1024], BF16)
    stg2 = [stage, stageB]
    PI = 3.14159265358979
    with contextlib.ExitStack() as ph:
        psb = lambda n, s, d: ph.enter_context(nc.sbuf_tensor("l0f" + n, s, d))
        add_ctx_attn(nc, cx, ph, p, pfx="f0")
        nw1 = psb("nw1", [128, D], F32)
        cosT2 = psb("cosT2", [128, T], BF16)
        sinsT2 = psb("sinsT2", [128, T], BF16)
        perm = psb("perm", [128, 128], BF16)
        rcol = psb("rcol", [128, 1], F32)
        invf = psb("invf", [128, 1], F32)
        sgn = psb("sgn", [128, 1], F32)
        p.g("gpsimd", lambda g: [
            g.iota(rcol[:], pattern=[[0, 1]], base=0, channel_multiplier=1,
                   allow_small_or_imprecise_dtypes=True)])
        ge64 = cx.col[:, 4:5]; r64 = cx.col[:, 5:6]; tmpc = cx.col[:, 6:7]
        p.c("vector", lambda v: v.tensor_single_scalar(ge64, rcol[:], 64.0, ALU.is_ge))
        p.c("vector", lambda v: v.scalar_tensor_tensor(r64, ge64, -64.0, rcol[:], ALU.mult, ALU.add))
        p.c("vector", lambda v: v.tensor_single_scalar(tmpc, r64, 32.0, ALU.is_ge))
        p.c("vector", lambda v: v.scalar_tensor_tensor(invf[:], tmpc, -32.0, r64, ALU.mult, ALU.add))
        p.c("scalar", lambda s: s.activation(invf[:], invf[:], AF.Exp, scale=-float(np.log(10000.0) / 32.0)))
        p.c("vector", lambda v: v.tensor_single_scalar(sgn[:], r64, 32.0, ALU.is_ge))
        p.c("vector", lambda v: v.tensor_scalar(sgn[:], sgn[:], 2.0, -1.0, ALU.mult, ALU.add))
        p.c("vector", lambda v: v.tensor_single_scalar(cx.tA[:, 0:1], r64, 32.0, ALU.add))
        p.c("vector", lambda v: v.tensor_single_scalar(cx.tB[:, 0:1], cx.tA[:, 0:1], 64.0, ALU.is_ge))
        p.c("vector", lambda v: v.scalar_tensor_tensor(cx.tA[:, 0:1], cx.tB[:, 0:1], -64.0, cx.tA[:, 0:1], ALU.mult, ALU.add))
        p.c("vector", lambda v: v.scalar_tensor_tensor(cx.tA[:, 0:1], ge64, 64.0, cx.tA[:, 0:1], ALU.mult, ALU.add))
        p.c("gpsimd", lambda g: g.iota(cx.tB[:, 0:128], pattern=[[1, 128]], base=0, channel_multiplier=0,
                                       allow_small_or_imprecise_dtypes=True))
        p.c("vector", lambda v: v.tensor_scalar(cx.tC[:, 0:128], cx.tB[:, 0:128], cx.tA[:, 0:1], None, ALU.is_equal))
        p.c("scalar", lambda s: s.copy(perm[:], cx.tC[:, 0:128]))
        for j0 in range(0, T, 1024):
            p.c("gpsimd", lambda g, j0=j0: g.iota(
                cx.tD[:, 0:1024], pattern=[[1, 1024]], base=j0, channel_multiplier=0,
                allow_small_or_imprecise_dtypes=True))
            for kind in range(2):
                ph0 = PI / 2 if kind == 0 else 0.0
                p.c("vector", lambda v, j0=j0: v.tensor_scalar(cx.tA[:, 0:1024], cx.tD[:, 0:1024],
                                                               invf[:], None, ALU.mult))
                if ph0:
                    p.c("vector", lambda v: v.tensor_single_scalar(cx.tA[:, 0:1024], cx.tA[:, 0:1024], PI / 2, ALU.add))
                p.c("vector", lambda v: v.tensor_scalar(cx.tB[:, 0:1024], cx.tA[:, 0:1024],
                                                        float(1.0 / (2 * PI)), 8388608.0 - 0.5, ALU.mult, ALU.add))
                p.c("vector", lambda v: v.tensor_single_scalar(cx.tB[:, 0:1024], cx.tB[:, 0:1024], -8388608.0, ALU.add))
                p.c("vector", lambda v: v.scalar_tensor_tensor(cx.tA[:, 0:1024], cx.tB[:, 0:1024], -2 * PI,
                                                               cx.tA[:, 0:1024], ALU.mult, ALU.add))
                p.c("vector", lambda v: v.tensor_single_scalar(cx.tB[:, 0:1024], cx.tA[:, 0:1024], PI, ALU.is_ge))
                p.c("vector", lambda v: v.scalar_tensor_tensor(cx.tA[:, 0:1024], cx.tB[:, 0:1024], -2 * PI,
                                                               cx.tA[:, 0:1024], ALU.mult, ALU.add))
                if kind == 0:
                    p.c("scalar", lambda s, j0=j0: s.activation(cosT2[:, j0:j0 + 1024], cx.tA[:, 0:1024], AF.Sin))
                else:
                    p.c("scalar", lambda s, j0=j0: s.activation(cx.tC[:, 0:1024], cx.tA[:, 0:1024], AF.Sin))
                    p.c("vector", lambda v, j0=j0: v.tensor_scalar(sinsT2[:, j0:j0 + 1024], cx.tC[:, 0:1024],
                                                                   sgn[:], None, ALU.mult))
        p.d("sync", lambda s: [s.dma_start(out=cx.tA[0:1, 0:D], in_=W["norm1_w"].rearrange("(a b) -> a b", a=1))], 1)
        cx.bcast_rows(p, nw1, cx.tA[0:1, 0:D], D)
        xn1T = psb("xn1T", [128, 8 * T], BF16)
        qkvT = psb("qkvT", [128, 8, 3072], BF16)
        qb_all = psb("qb_all", [128, 24], F32)
        qT = psb("qT0", [128, T], BF16)
        kT = psb("kT0", [128, T], BF16)
        vaug = psb("vaug0", [128, 2, T // 128, 65], BF16)
        qraw = psb("qraw", [128, 512], BF16)
        vbt = psb("vbt", [128, 128], F32)
        for tt in range(T // 128):
            p.d("sync", lambda s, tt=tt: [s.dma_start(out=cx.tA[:, 0:D],
                                                      in_=x_d[tt * 128:(tt + 1) * 128, :])], 1)
            cx.layernorm_tile(p, xbf[:, 0:D], cx.tA[:, 0:D], D, 1e-6, g_tile=nw1[:, 0:D], rms=True)
            dst = xn1T[:, :].rearrange("p (n t) -> p n t", n=8)[:, :, tt * 128:(tt + 1) * 128]
            cx.transpose_tile(p, dst, xbf[:, 0:D], D)
        load_transposed(nc, cx, p, qkvT, W["qkv_w"], 3072, 1024, stg2)
        p.d("gpsimd", lambda g: [g.dma_start(out=qb_all[:], in_=W["qkv_b"].rearrange("(n p) -> p n", p=128),
                                             allow_slow_non_contiguous=True)], 1)
        for hp in range(8):
            for which, dstT, col0 in ((0, qT, hp), (1, kT, 8 + hp)):
                for tch in range(T // 512):
                    def mms(t, col0=col0, tch=tch):
                        return [t.matmul(cx.P0[:, 0:512], qkvT[:, dc, col0 * 128:(col0 + 1) * 128],
                                         xn1T[:, dc * T + tch * 512:dc * T + (tch + 1) * 512],
                                         start=(dc == 0), stop=(dc == 7)) for dc in range(8)]
                    p.g("tensor", mms)
                    p.c("scalar", lambda s, col0=col0: s.add(qraw[:], cx.P0[:, 0:512],
                                                             qb_all[:, col0:col0 + 1]))
                    p.c("tensor", lambda t: t.matmul(cx.P1[0:128, 0:512], perm[:], qraw[:],
                                                     start=True, stop=True))
                    p.c("vector", lambda v, tch=tch: v.tensor_mul(cx.tA[:, 0:512], qraw[:],
                                                                  cosT2[:, tch * 512:(tch + 1) * 512]))
                    p.c("vector", lambda v, tch=tch: v.tensor_mul(cx.tB[:, 0:512], cx.P1[:, 0:512],
                                                                  sinsT2[:, tch * 512:(tch + 1) * 512]))
                    p.c("vector", lambda v, dstT=dstT, tch=tch: v.tensor_add(
                        dstT[:, tch * 512:(tch + 1) * 512], cx.tA[:, 0:512], cx.tB[:, 0:512]))
            p.c("gpsimd", lambda g: g.memset(vaug[:, :, :, 64:65], 1.0))
            p.d("sync", lambda s, hp=hp: [s.dma_start(
                out=cx.tD[0:1, 0:128],
                in_=W["qkv_b"].rearrange("(a b) -> a b", a=1)[:, 2048 + hp * 128:2048 + (hp + 1) * 128])], 1)
            cx.bcast_rows(p, vbt, cx.tD[0:1, 0:128], 128)
            for tc in range(T // 128):
                def mmsv(t, hp=hp, tc=tc):
                    return [t.matmul(cx.P2[:, 0:128], xn1T[:, dc * T + tc * 128:dc * T + (tc + 1) * 128],
                                     qkvT[:, dc, 2048 + hp * 128:2048 + (hp + 1) * 128],
                                     start=(dc == 0), stop=(dc == 7)) for dc in range(8)]
                p.g("tensor", mmsv)
                p.c("vector", lambda v: v.tensor_add(cx.tA[:, 0:128], cx.P2[:, 0:128], vbt[:]))
                p.c("scalar", lambda s, tc=tc: s.copy(vaug[:, 0, tc, 0:64], cx.tA[:, 0:64]))
                p.c("scalar", lambda s, tc=tc: s.copy(vaug[:, 1, tc, 0:64], cx.tA[:, 64:128]))
            attn_one_hp(nc, cx, p, hp, qT, kT, vaug, attn_dram, T, False, None)

    with contextlib.ExitStack() as ph:
        psb = lambda n, s, d: ph.enter_context(nc.sbuf_tensor("l0o" + n, s, d))
        owT = psb("owT", [128, 8, 1024], BF16)
        obt = psb("obt", [128, D], F32)
        atile = psb("atile0", [128, 8, 128], BF16)
        load_transposed(nc, cx, p, owT, W["out_w"], 1024, 1024, stg2)
        p.d("sync", lambda s: [s.dma_start(out=cx.tA[0:1, 0:D], in_=W["out_b"].rearrange("(a b) -> a b", a=1))], 1)
        cx.bcast_rows(p, obt, cx.tA[0:1, 0:D], D)
        for tc in range(T // 128):
            p.d("sync", lambda s, tc=tc: [s.dma_start(
                out=atile[:],
                in_=attn_dram.rearrange("(n p) t -> p n t", p=128)[:, :, tc * 128:(tc + 1) * 128])], 1)
            def mms(t, tc=tc):
                out = []
                for hpc in range(8):
                    for cc in range(2):
                        out.append(t.matmul(cx.P1[:, cc * 512:(cc + 1) * 512],
                                            atile[:, hpc, :],
                                            owT[:, hpc, cc * 512:(cc + 1) * 512],
                                            start=(hpc == 0), stop=(hpc == 7)))
                return out
            p.g("tensor", mms)
            p.d("sync", lambda s, tc=tc: [s.dma_start(
                out=cx.tB[:, 0:D], in_=x_d[tc * 128:(tc + 1) * 128, :])], 1)
            p.c("vector", lambda v: v.tensor_add(cx.tA[:, 0:D], cx.tB[:, 0:D], cx.P1[:, 0:D]))
            p.c("vector", lambda v: v.tensor_add(cx.tA[:, 0:D], cx.tA[:, 0:D], obt[:, 0:D]))
            p.d("sync", lambda s, tc=tc: [s.dma_start(
                out=x1_dram[tc * 128:(tc + 1) * 128, :], in_=cx.tA[:, 0:D])], 1)

    with contextlib.ExitStack() as ph:
        psb = lambda n, s, d: ph.enter_context(nc.sbuf_tensor("l0m" + n, s, d))
        xn2T = psb("xn2T", [128, 8 * T], BF16)
        nw2 = psb("nw2", [128, D], F32)
        gateT = psb("gateT0", [128, 8, 8], BF16)
        l8 = psb("l8", [128, 8], F32)
        e80 = psb("e80", [128, 8], F32)
        p.d("sync", lambda s: [s.dma_start(out=cx.tA[0:1, 0:D], in_=W["norm2_w"].rearrange("(a b) -> a b", a=1))], 1)
        cx.bcast_rows(p, nw2, cx.tA[0:1, 0:D], D)
        for tt in range(T // 128):
            p.d("sync", lambda s, tt=tt: [s.dma_start(out=cx.tA[:, 0:D],
                                                      in_=x1_dram[tt * 128:(tt + 1) * 128, :])], 1)
            cx.layernorm_tile(p, xbf[:, 0:D], cx.tA[:, 0:D], D, 1e-6, g_tile=nw2[:, 0:D], rms=True)
            dst = xn2T[:, :].rearrange("p (n t) -> p n t", n=8)[:, :, tt * 128:(tt + 1) * 128]
            cx.transpose_tile(p, dst, xbf[:, 0:D], D)
        p.d("gpsimd", lambda g: [g.dma_start(out=stage[0:8, 0:1024], in_=W["gate_w"][:, :])], 1)
        for dc in range(8):
            p.c("tensor", lambda t, dc=dc: t.transpose(cx.P3[:, 0:8], stage[0:8, dc * 128:(dc + 1) * 128], cx.ident[0:8, 0:8]))
            p.c("scalar", lambda s, dc=dc: s.copy(gateT[:, dc, :], cx.P3[:, 0:8]))
        t8a = cx.col[:, 0:1]; t8b = cx.col[:, 1:2]; t8c = cx.col[:, 2:3]; t8d = cx.col[:, 3:4]
        for tt in range(T // 128):
            def mms(t, tt=tt):
                return [t.matmul(cx.P4[:, 0:8], xn2T[:, dc * T + tt * 128:dc * T + (tt + 1) * 128],
                                 gateT[:, dc, :], start=(dc == 0), stop=(dc == 7)) for dc in range(8)]
            p.g("tensor", mms)
            p.c("vector", lambda v: v.tensor_copy(l8[:], cx.P4[:, 0:8]))
            p.c("vector", lambda v: v.reduce_max(t8a, l8[:], axis=AX.X, negate=True))
            p.c("vector", lambda v: v.tensor_scalar(e80[:], l8[:], t8a, None, ALU.add))
            p.c("vector", lambda v: v.tensor_single_scalar(cx.tA[:, 0:8], e80[:], 0.0, ALU.is_ge))
            p.c("vector", lambda v: v.scalar_tensor_tensor(cx.tA[:, 0:8], cx.tA[:, 0:8], -1e30, l8[:], ALU.mult, ALU.add))
            p.c("vector", lambda v: v.reduce_max(t8b, cx.tA[:, 0:8], axis=AX.X))
            p.c("vector", lambda v: v.tensor_scalar(cx.tB[:, 0:8], l8[:], t8b, None, ALU.is_ge))
            p.c("scalar", lambda s: s.activation(e80[:], e80[:], AF.Exp))
            p.c("vector", lambda v: v.tensor_add(t8c, t8b, t8a))
            p.c("scalar", lambda s: s.activation(t8c, t8c, AF.Exp))
            p.c("vector", lambda v: v.tensor_scalar(t8c, t8c, cx.kcol[:, 0:1], None, ALU.add))
            p.c("vector", lambda v: v.reciprocal(t8d, t8c))
            p.c("vector", lambda v: v.tensor_mul(e80[:], e80[:], cx.tB[:, 0:8]))
            p.c("vector", lambda v, tt=tt: v.tensor_scalar(wtab[:, tt, :], e80[:], t8d, None, ALU.mult))
        for tt in range(T // 128):
            p.d("sync", lambda s, tt=tt: [
                s.dma_start(out=cx.tA[:, 0:D], in_=x_d[tt * 128:(tt + 1) * 128, :]),
                s.dma_start(out=cx.tB[:, 0:D], in_=x1_dram[tt * 128:(tt + 1) * 128, :])], 2)
            p.c("vector", lambda v: v.tensor_add(cx.tC[:, 0:D], cx.tA[:, 0:D], cx.tB[:, 0:D]))
            p.d("sync", lambda s, tt=tt: [s.dma_start(out=acc_dram[tt * 128:(tt + 1) * 128, :],
                                                      in_=cx.tC[:, 0:D])], 1)
        with contextlib.ExitStack() as ph2:
            psb2 = lambda n, s, d: ph2.enter_context(nc.sbuf_tensor("l0e" + n, s, d))
            w1eT = psb2("w1eT0", [128, 8, 2048], BF16)
            w2eT = psb2("w2eT0", [128, 8, 2048], BF16)
            w3eT = psb2("w3eT0", [128, 16, 1024], BF16)
            b1c = psb2("b1c", [128, 16], F32)
            b2c = psb2("b2c", [128, 16], F32)
            b3t = nw2
            hTg = psb2("hTg", [128, 16, 512], BF16)
            for e in range(8):
                load_transposed(nc, cx, p, w1eT, W["w1"][e], 2048, 1024, stg2)
                load_transposed(nc, cx, p, w2eT, W["w2"][e], 2048, 1024, stg2)
                load_transposed(nc, cx, p, w3eT, W["w3"][e], 1024, 2048, [stage, stage])
                p.d("gpsimd", lambda g, e=e: [
                    g.dma_start(out=b1c[:], in_=W["b1"][e].rearrange("(n p) -> p n", p=128),
                                allow_slow_non_contiguous=True),
                    g.dma_start(out=b2c[:], in_=W["b2"][e].rearrange("(n p) -> p n", p=128),
                                allow_slow_non_contiguous=True),
                    g.dma_start(out=cx.tD[0:1, 0:D], in_=W["b3"][e].rearrange("(a b) -> a b", a=1))], 3)
                cx.bcast_rows(p, b3t, cx.tD[0:1, 0:D], D)
                for g8 in range(T // 512):
                    g0 = g8 * 512
                    for hc in range(16):
                        def mms1(t, hc=hc, g0=g0):
                            return [t.matmul(cx.P0[:, 0:512], w1eT[:, dc, hc * 128:(hc + 1) * 128],
                                             xn2T[:, dc * T + g0:dc * T + g0 + 512],
                                             start=(dc == 0), stop=(dc == 7)) for dc in range(8)]
                        p.g("tensor", mms1)
                        def mms2(t, hc=hc, g0=g0):
                            return [t.matmul(cx.P1[:, 0:512], w2eT[:, dc, hc * 128:(hc + 1) * 128],
                                             xn2T[:, dc * T + g0:dc * T + g0 + 512],
                                             start=(dc == 0), stop=(dc == 7)) for dc in range(8)]
                        p.g("tensor", mms2)
                        p.c("scalar", lambda s, hc=hc: s.activation(cx.tA[:, 0:512], cx.P0[:, 0:512],
                                                                    AF.Silu, bias=b1c[:, hc:hc + 1]))
                        p.c("scalar", lambda s, hc=hc: s.add(cx.tB[:, 0:512], cx.P1[:, 0:512],
                                                             b2c[:, hc:hc + 1]))
                        p.c("vector", lambda v, hc=hc: v.tensor_mul(hTg[:, hc, :], cx.tA[:, 0:512], cx.tB[:, 0:512]))
                    for ttl in range(4):
                        tglob = g8 * 4 + ttl
                        def mms3(t, ttl=ttl):
                            out = []
                            for hc in range(16):
                                for cc in range(2):
                                    out.append(t.matmul(cx.P2[:, cc * 512:(cc + 1) * 512],
                                                        hTg[:, hc, ttl * 128:(ttl + 1) * 128],
                                                        w3eT[:, hc, cc * 512:(cc + 1) * 512],
                                                        start=(hc == 0), stop=(hc == 15)))
                            return out
                        p.g("tensor", mms3)
                        p.c("vector", lambda v: v.tensor_add(cx.tA[:, 0:D], cx.P2[:, 0:D], b3t[:, 0:D]))
                        p.c("vector", lambda v, tglob=tglob, e=e: v.tensor_scalar(
                            cx.tB[:, 0:D], cx.tA[:, 0:D], wtab[:, tglob, e:e + 1], None, ALU.mult))
                        p.d("gpsimd", lambda g, tglob=tglob: [g.dma_start(
                            out=acc_dram[tglob * 128:(tglob + 1) * 128, :], in_=cx.tB[:, 0:D],
                            accum_op=ALU.add)], 1)
        for tt in range(T // 128):
            p.d("sync", lambda s, tt=tt: [s.dma_start(out=cx.tA[:, 0:D],
                                                      in_=acc_dram[tt * 128:(tt + 1) * 128, :])], 1)
            cx.layernorm_tile(p, cx.tB[:, 0:D], cx.tA[:, 0:D], D, 1e-5)
            p.d("sync", lambda s, tt=tt: [s.dma_start(out=y_d[tt * 128:(tt + 1) * 128, :],
                                                      in_=cx.tB[:, 0:D])], 1)


def build_l0_ffn():
    nc = bass.Bass("TRN2", target_bir_lowering=False, debug=False)
    x_d = nc.dram_tensor("x", [T, D], F32, kind="ExternalInput").ap()
    W = {"norm1_w": nc.dram_tensor("norm1_w", [D], F32, kind="ExternalInput").ap(),
         "norm2_w": nc.dram_tensor("norm2_w", [D], F32, kind="ExternalInput").ap(),
         "qkv_w": nc.dram_tensor("qkv_w", [3 * D, D], F32, kind="ExternalInput").ap(),
         "qkv_b": nc.dram_tensor("qkv_b", [3 * D], F32, kind="ExternalInput").ap(),
         "out_w": nc.dram_tensor("out_w", [D, D], F32, kind="ExternalInput").ap(),
         "out_b": nc.dram_tensor("out_b", [D], F32, kind="ExternalInput").ap(),
         "gate_w": nc.dram_tensor("gate_w", [8, D], F32, kind="ExternalInput").ap(),
         "w1": nc.dram_tensor("w1", [8, 2048, D], F32, kind="ExternalInput").ap(),
         "b1": nc.dram_tensor("b1", [8, 2048], F32, kind="ExternalInput").ap(),
         "w2": nc.dram_tensor("w2", [8, 2048, D], F32, kind="ExternalInput").ap(),
         "b2": nc.dram_tensor("b2", [8, 2048], F32, kind="ExternalInput").ap(),
         "w3": nc.dram_tensor("w3", [8, D, 2048], F32, kind="ExternalInput").ap(),
         "b3": nc.dram_tensor("b3", [8, D], F32, kind="ExternalInput").ap()}
    y_d = nc.dram_tensor("y", [T, D], F32, kind="ExternalOutput").ap()
    with contextlib.ExitStack() as stack:
        cx = Ctx(nc, stack)
        build_program(nc, lambda p: (cx.setup(p), make_l0_ffn_steps(nc, cx, stack, p, x_d, y_d, W)))
    return nc


_cache = {}



def build_full(plan=None):
    nc = bass.Bass("TRN2", target_bir_lowering=False, debug=False)
    x_d = nc.dram_tensor("L0_attn_x", [T, D], F32, kind="ExternalInput").ap()
    y_d = nc.dram_tensor("y", [T, D], F32, kind="ExternalOutput").ap()
    t01 = nc.dram_tensor("t01", [T, D], F32, kind="Internal").ap()
    t12 = nc.dram_tensor("t12", [T, D], F32, kind="Internal").ap()
    t23 = nc.dram_tensor("t23", [T, D], F32, kind="Internal").ap()

    def dt_(n, s):
        return nc.dram_tensor(n, s, F32, kind="ExternalInput").ap()

    W0a = {"ln_g": dt_("L0_attn_ln_g", [D]), "ln_b": dt_("L0_attn_ln_b", [D]),
           "W_in": dt_("L0_attn_W_in", [D, 2 * DI]), "conv_w": dt_("L0_attn_conv_w", [DI, 4]),
           "conv_b": dt_("L0_attn_conv_b", [DI]), "W_x": dt_("L0_attn_W_x", [DI, 2 * DS]),
           "W_dt": dt_("L0_attn_W_dt", [DI, DI]), "b_dt": dt_("L0_attn_b_dt", [DI]),
           "A_log": dt_("L0_attn_A_log", [DS]), "D": dt_("L0_attn_D", [DI]),
           "W_out": dt_("L0_attn_W_out", [DI, D])}
    W0f = {"norm1_w": dt_("L0_ffn_norm1_w", [D]), "norm2_w": dt_("L0_ffn_norm2_w", [D]),
           "qkv_w": dt_("L0_ffn_qkv_w", [3 * D, D]), "qkv_b": dt_("L0_ffn_qkv_b", [3 * D]),
           "out_w": dt_("L0_ffn_out_w", [D, D]), "out_b": dt_("L0_ffn_out_b", [D]),
           "gate_w": dt_("L0_ffn_gate_w", [8, D]), "w1": dt_("L0_ffn_w1", [8, 2048, D]),
           "b1": dt_("L0_ffn_b1", [8, 2048]), "w2": dt_("L0_ffn_w2", [8, 2048, D]),
           "b2": dt_("L0_ffn_b2", [8, 2048]), "w3": dt_("L0_ffn_w3", [8, D, 2048]),
           "b3": dt_("L0_ffn_b3", [8, D])}
    W1a = {"Wq": dt_("L1_attn_Wq", [D, 16, 64]), "Wkv_down": dt_("L1_attn_Wkv_down", [D, 256]),
           "Wk_up": dt_("L1_attn_Wk_up", [256, 16, 64]), "Wv_up": dt_("L1_attn_Wv_up", [256, 16, 64]),
           "Wout": dt_("L1_attn_Wout", [16, 64, D])}
    W1f = {"gate": dt_("L1_ffn_gate", [8, D]), "corr_bias": dt_("L1_ffn_corr_bias", [8]),
           "w1": dt_("L1_ffn_w1", [8, 512, 512]), "w2": dt_("L1_ffn_w2", [8, 512, 512]),
           "fc1_w": dt_("L1_ffn_fc1_w", [512, D]), "fc2_w": dt_("L1_ffn_fc2_w", [D, 512]),
           "up_w": dt_("L1_ffn_up_w", [4096, D]), "down_w": dt_("L1_ffn_down_w", [D, 4096])}

    with contextlib.ExitStack() as stack:
        cx = Ctx(nc, stack)

        def all_steps(p):
            cx.setup(p)
            with contextlib.ExitStack() as s0:
                make_l0_attn_steps(nc, cx, s0, p, x_d, t01, W0a)
            with contextlib.ExitStack() as s1:
                make_l0_ffn_steps(nc, cx, s1, p, t01, t12, W0f)
            with contextlib.ExitStack() as s2:
                make_l1_attn_steps(nc, cx, s2, p, t12, t23, W1a)
            with contextlib.ExitStack() as s3:
                make_l1_ffn_steps(nc, cx, s3, p, t23, y_d, W1f)

        if plan is None:
            meta, caps = build_program(nc, all_steps, record=True)
            return _compute_schedule(meta, caps)
        build_program(nc, all_steps, plan=plan)
    return nc


def build_full_sched():
    plan = build_full(None)
    return build_full(plan)


def _get(name, builder):
    if name not in _cache:
        _cache[name] = builder()
    return _cache[name]


def kernel(**inputs):
    n = 8
    nc = _get("full", build_full_sched)
    maps = []
    for b in range(n):
        m = {k: np.ascontiguousarray(v) for k, v in inputs.items() if k != "L0_attn_x"}
        m["L0_attn_x"] = np.ascontiguousarray(inputs["L0_attn_x"][b])
        maps.append(m)
    res = run_bass_kernel_spmd(nc, maps, core_ids=list(range(n)))
    return np.stack([r["y"] for r in res.results], axis=0)
